# version 18
# baseline (speedup 1.0000x reference)
import numpy as np
import ml_dtypes
import contextlib
import os as _os
import concourse.bass as bass
import concourse.mybir as mybir
from concourse.bass import broadcast_tensor_aps
from concourse.bass_utils import run_bass_kernel_spmd

F32 = mybir.dt.float32
BF16 = mybir.dt.bfloat16
AF = mybir.ActivationFunctionType
ALU = mybir.AluOpType
AX = mybir.AxisListType
BFNP = ml_dtypes.bfloat16

D = 1024
HD = 4096
NH = 16
DH = 64
CH = 128
TILE = 256
LN_EPS = 1e-5
RMS_EPS = 1e-6
LOG_QSCALE = float(np.log(np.float64(DH) ** -0.5))

ENGS = ("sync", "tensor", "vector", "scalar", "gpsimd")


class Op:
    __slots__ = ("eng", "fn", "deps", "dma", "inc", "sem_val", "idx", "tag", "slot")

    def __init__(self, eng, fn, dma, idx, tag=""):
        self.eng = eng
        self.fn = fn
        self.deps = []
        self.dma = dma
        self.inc = False
        self.sem_val = None
        self.idx = idx
        self.tag = tag
        self.slot = None


class Buf:
    __slots__ = ("w", "r")

    def __init__(self):
        self.w = None
        self.r = []


NSLOT = 8


class Prog:
    def __init__(self, nc):
        self.nc = nc
        self.ops = []
        self.eng_ops = {e: [] for e in ENGS}
        self.sems = {
            "tensor": nc.alloc_semaphore("sPE"),
            "vector": nc.alloc_semaphore("sVE"),
            "scalar": nc.alloc_semaphore("sSE"),
            "gpsimd": nc.alloc_semaphore("sGP"),
        }
        for q in ("sync", "gpsimd"):
            for i in range(NSLOT):
                self.sems[f"{q}_dma{i}"] = nc.alloc_semaphore(f"s_{q}_d{i}")
        self._dma_seq = {"sync": 0, "gpsimd": 0}
        self._last_per_stream = {}
        self._barrier = None
        self._barrier_seen = set()

    def _stream(self, op):
        if op.dma:
            return f"{op.eng}_dma{op.slot}"
        return op.eng

    def emit(self, eng, fn, reads=(), writes=(), deps=(), dma=False, tag=""):
        o = Op(eng, fn, dma, len(self.ops), tag)
        if dma:
            o.slot = self._dma_seq[eng] % NSLOT
            self._dma_seq[eng] += 1
        self._last_per_stream[self._stream(o)] = o
        dl = list(deps)
        for b in reads:
            if b.w is not None:
                dl.append(b.w)
        for b in writes:
            dl.extend(b.r)
            if b.w is not None:
                dl.append(b.w)
        if self._barrier is not None and eng not in self._barrier_seen:
            dl.extend(self._barrier)
            self._barrier_seen.add(eng)
        o.deps = [d for d in dl if d is not None and d is not o]
        self.ops.append(o)
        self.eng_ops[eng].append(o)
        for b in reads:
            b.r.append(o)
        for b in writes:
            b.w = o
            b.r = []
        return o

    def barrier(self):
        self._barrier = list(self._last_per_stream.values())
        self._barrier_seen = set()

    def finalize(self):
        for o in self.ops:
            best = {}
            for d in o.deps:
                if d.eng == o.eng and not d.dma and d.eng in ("tensor", "sync"):
                    continue
                s = self._stream(d)
                if s not in best or d.idx > best[s].idx:
                    best[s] = d
            o.deps = list(best.values())
            for d in o.deps:
                d.inc = True
        counts = {s: 0 for s in self.sems}
        for e in ENGS:
            for o in self.eng_ops[e]:
                if o.dma:
                    o.inc = True
                if o.inc:
                    s = self._stream(o)
                    counts[s] += 16 if o.dma else 1
                    o.sem_val = counts[s]
        self.final_counts = counts

    def replay_engine(self, eng_name, eng):
        waited = {}
        for o in self.eng_ops[eng_name]:
            for d in sorted(o.deps, key=lambda x: x.idx):
                s = self._stream(d)
                v = d.sem_val
                if waited.get(s, 0) < v:
                    eng.wait_ge(self.sems[s], v)
                    waited[s] = v
            if o.dma:
                s = self._stream(o)
                v = o.sem_val - 16
                if v > 0 and waited.get(s, 0) < v:
                    eng.wait_ge(self.sems[s], v)
                    waited[s] = v
            inst = o.fn(eng)
            if o.inc:
                inst.then_inc(self.sems[self._stream(o)], 16 if o.dma else 1)


def bcast_in1(in1_ap, like_ap):
    b, _ = broadcast_tensor_aps(in1_ap, like_ap)
    return b


def build_program(n_tok, audit=False, debug=False, phases="0ABC", blevel=99):
    assert n_tok % TILE == 0
    seq_len = min(n_tok, 4096)
    n_tiles = n_tok // TILE
    tiles_per_seq = seq_len // TILE
    n_cs = n_tok // CH

    nc = bass.Bass("TRN2", target_bir_lowering=False, detect_race_conditions=audit)
    p = Prog(nc)
    p.audit = audit

    x_d = nc.dram_tensor("x", [n_tok, D], F32, kind="ExternalInput").ap()
    wq_d = nc.dram_tensor("wq", [D, D], F32, kind="ExternalInput").ap()
    wk_d = nc.dram_tensor("wk", [D, D], F32, kind="ExternalInput").ap()
    wv_d = nc.dram_tensor("wv", [D, D], F32, kind="ExternalInput").ap()
    wg_d = nc.dram_tensor("wg", [D, D], F32, kind="ExternalInput").ap()
    wgt_d = nc.dram_tensor("wgt", [D, NH], F32, kind="ExternalInput").ap()
    wo_d = nc.dram_tensor("wo", [D, D], F32, kind="ExternalInput").ap()
    w1_d = nc.dram_tensor("w1", [HD, D], F32, kind="ExternalInput").ap()
    b1_d = nc.dram_tensor("b1", [HD], F32, kind="ExternalInput").ap()
    w2_d = nc.dram_tensor("w2", [D, HD], F32, kind="ExternalInput").ap()
    b2_d = nc.dram_tensor("b2", [D], F32, kind="ExternalInput").ap()
    identbf_d = nc.dram_tensor("c_identbf", [128, 128], BF16, kind="ExternalInput").ap()
    identf_d = nc.dram_tensor("c_identf", [128, 128], F32, kind="ExternalInput").ap()
    tril_d = nc.dram_tensor("c_tril", [128, 128], F32, kind="ExternalInput").ap()
    cumu_d = nc.dram_tensor("c_cumu", [128, 128], F32, kind="ExternalInput").ap()
    out_d = nc.dram_tensor("out", [n_tok, D], F32, kind="ExternalOutput").ap()
    _scr_kind = "ExternalOutput" if _os.environ.get("KSCROUT") else "Internal"
    x1_d = nc.dram_tensor("x1_scr", [n_tok, D], F32, kind=_scr_kind).ap()
    x2_d = nc.dram_tensor("x2_scr", [n_tok, D], F32, kind=_scr_kind).ap()
    bw1t_d = nc.dram_tensor("bw1t_scr", [128, 8 * HD], BF16, kind=_scr_kind).ap()
    if _os.environ.get("KSCROUT"):
        st_d = nc.dram_tensor("stats_out", [128, 256], F32, kind="ExternalOutput").ap()
        ff_d = nc.dram_tensor("ff_out", [8, 128, 2 * D], BF16, kind="ExternalOutput").ap()
    if debug:
        dbg_d = nc.dram_tensor("dbg", [64, 128, 2 * D], F32, kind="ExternalOutput").ap()
        dbg_bf = nc.dram_tensor("dbgb", [64, 128, 2 * D], BF16, kind="ExternalOutput").ap()
    bw2t_d = nc.dram_tensor("bw2t_scr", [128, 32 * D], BF16, kind=_scr_kind).ap()

    stP = contextlib.ExitStack()

    def sbP(name, shape, dt):
        return stP.enter_context(nc.sbuf_tensor(name, shape, dt))

    ident_bf = sbP("ident_bf", [128, 128], BF16)
    ident_f = sbP("ident_f", [128, 128], F32)
    tril_f = sbP("tril_f", [128, 128], F32)
    cumu_f = sbP("cumu_f", [128, 128], F32)
    onesP_bf = sbP("onesP_bf", [128, 1], BF16)
    onesP16_f = sbP("onesP16_f", [128, 1], F32)
    ones1_bf = sbP("ones1_bf", [1, 128], BF16)
    ones1_f = sbP("ones1_f", [1, 128], F32)
    onesrow_bf = sbP("onesrow_bf", [1, TILE], BF16)
    m1_sb = sbP("m1", [128, n_cs], F32)
    r1_sb = sbP("r1", [128, n_cs], F32)
    m3_sb = sbP("m3", [128, n_cs], F32)
    r3_sb = sbP("r3", [128, n_cs], F32)
    s13_sb = sbP("s13", [128, n_cs], F32)
    s23_sb = sbP("s23", [128, n_cs], F32)
    S_sb = sbP("S_state", [64, NH * DH], F32)
    S_bf = sbP("S_state_bf", [64, NH * DH], BF16)
    cvals = sbP("cvals", [128, 4], F32)
    b1_cols = sbP("b1_cols", [128, 32], F32)
    pool_bf = sbP("pool_bf", [128, 72384], BF16)
    pool_f = sbP("pool_f", [128, 13568], F32)

    def bfv(base, n, rows=None):
        if rows is None:
            return pool_bf[:, base:base + n]
        return pool_bf[0:rows, base:base + n]

    def f32v(base, n, rows=None):
        if rows is None:
            return pool_f[:, base:base + n]
        return pool_f[0:rows, base:base + n]
    b2_bf = sbP("b2_bf", [1, D], BF16)
    fin_sb = sbP("fin", [128, n_cs], F32)
    eBbc_sb = sbP("eBbc", [128, NH], F32)
    pbig0 = stP.enter_context(nc.psum_tensor("pbig0", [128, 1024], F32))
    pbig1 = stP.enter_context(nc.psum_tensor("pbig1", [128, 1024], F32))
    phalf0 = stP.enter_context(nc.psum_tensor("phalf0", [128, 512], F32))
    phalf1 = stP.enter_context(nc.psum_tensor("phalf1", [128, 512], F32))
    ptp = stP.enter_context(nc.psum_tensor("ptp", [128, 1024], BF16))
    ptiny = stP.enter_context(nc.psum_tensor("ptiny", [128, 512], F32))

    consts = Buf()
    tinyb0 = Buf()
    stats1 = Buf()
    stats3 = Buf()
    Sbuf = Buf()
    bias_bf = Buf()
    finb = Buf()

    p.emit("sync", lambda e: e.dma_start(out=ident_bf[:], in_=identbf_d), writes=[consts], dma=True)
    p.emit("sync", lambda e: e.dma_start(out=ident_f[:], in_=identf_d), writes=[consts], dma=True)
    p.emit("sync", lambda e: e.dma_start(out=tril_f[:], in_=tril_d), writes=[consts], dma=True)
    p.emit("sync", lambda e: e.dma_start(out=cumu_f[:], in_=cumu_d), writes=[consts], dma=True)
    p.emit("gpsimd", lambda e: e.dma_start(out=b2_bf[:], in_=b2_d[None, :]), writes=[bias_bf], dma=True)
    p.emit("vector", lambda e: e.memset(onesP_bf[:], 1.0), writes=[consts])
    p.emit("vector", lambda e: e.memset(onesP16_f[:], 1.0 / 16.0), writes=[consts])
    p.emit("vector", lambda e: e.memset(ones1_bf[:], 1.0), writes=[consts])
    p.emit("vector", lambda e: e.memset(ones1_f[:], 1.0), writes=[consts])
    p.emit("vector", lambda e: e.memset(onesrow_bf[:], 1.0), writes=[consts])
    p.emit("vector", lambda e: e.memset(S_sb[:], 0.0), writes=[Sbuf])
    p.emit("vector", lambda e: e.memset(S_bf[:], 0.0), writes=[Sbuf])
    _cv = [float(LN_EPS), 1.0, float(LOG_QSCALE), float(RMS_EPS)]
    for _i, _v in enumerate(_cv):
        p.emit("vector", (lambda _i=_i, _v=_v: lambda e: e.memset(cvals[:, _i:_i + 1], _v))(), writes=[consts])
        nc.const_aps.register(F32, _v, cvals[:, _i:_i + 1])

    def emit_stat_finish(s1g, s2g, gsrc, mg, rg, gdst, n):
        p.emit("vector", lambda e: e.tensor_scalar(mg[:, 0:n], s1g[:, 0:n], 1.0 / D, None, ALU.mult),
               reads=[gsrc], writes=[gdst])
        p.emit("vector", lambda e: e.tensor_tensor(fin_sb[:, 0:n], mg[:, 0:n], mg[:, 0:n], ALU.mult),
               reads=[gdst], writes=[finb])
        p.emit("vector", lambda e: e.scalar_tensor_tensor(fin_sb[:, 0:n], s2g[:, 0:n], 1.0 / D, fin_sb[:, 0:n], ALU.mult, ALU.subtract),
               reads=[gsrc, finb], writes=[finb])
        p.emit("scalar", lambda e: e.activation(fin_sb[:, 0:n], fin_sb[:, 0:n], AF.Ln, bias=float(LN_EPS), scale=1.0),
               reads=[finb], writes=[finb])
        p.emit("scalar", lambda e: e.activation(rg[:, 0:n], fin_sb[:, 0:n], AF.Exp, bias=0.0, scale=-0.5),
               reads=[finb], writes=[gdst])

    def emit_ffn_phase(src_d, dst_d, mg, rg, statbuf, bw1t, bw1b, bw2t, bw2b, tagp):
        xsb = [f32v(0, 2048), f32v(2048, 2048)]
        xsbb = [Buf() for _ in range(2)]
        xln = bfv(65536, 2048)
        xlnb = Buf()
        xfmB = [bfv(67584, 2048), bfv(69632, 2048)]
        xfmbB = [Buf(), Buf()]
        hbf = [bfv(71680, 256), bfv(71936, 256)]
        hbfb = [Buf() for _ in range(2)]
        x1o = [f32v(4096, 2048), f32v(6144, 2048)]
        x1ob = [Buf() for _ in range(2)]
        hp = [phalf0, phalf1]
        hpb = [Buf() for _ in range(2)]
        op_ps = [pbig0, pbig1]
        opb = [Buf() for _ in range(2)]
        tp2 = ptp
        tp2b = Buf()

        for t in range(n_tiles):
            bi = t % 2
            xb, xt = xsbb[bi], xsb[bi]
            ob, ot = x1ob[bi], x1o[bi]
            xfm, xfmb = xfmB[bi], xfmbB[bi]
            tok0 = t * TILE
            p.emit("sync",
                   (lambda xt=xt, tok0=tok0: lambda e: e.dma_start(
                       out=xt[:].rearrange("p (c f) -> p c f", c=2), in_=src_d[tok0:tok0 + TILE, :].rearrange("(c p) f -> p c f", c=2)))(),
                   writes=[xb], dma=True, tag=f"{tagp}ld{t}")
            for c in range(2):
                tc = t * 2 + c
                p.emit("vector",
                       (lambda xt=xt, c=c, tc=tc: lambda e: e.tensor_scalar(
                           xln[:, c * D:(c + 1) * D], xt[:, c * D:(c + 1) * D],
                           mg[:, tc:tc + 1], rg[:, tc:tc + 1], ALU.subtract, ALU.mult))(),
                       reads=[xb, statbuf], writes=[xlnb], tag=f"{tagp}ln{t}_{c}")
            for c in range(2):
                for kc in range(8):
                    p.emit("tensor",
                           (lambda c=c, kc=kc: lambda e: e.transpose(
                               tp2[:, kc * 128:(kc + 1) * 128],
                               xln[:, c * D + kc * 128:c * D + (kc + 1) * 128], ident_bf[:]))(),
                           reads=[xlnb, consts], writes=[tp2b], tag=f"{tagp}tp{t}_{c}_{kc}")
                p.emit("scalar",
                       (lambda c=c: lambda e: e.copy(
                           xfm[:].rearrange("p (kc tt) -> p kc tt", kc=8)[:, :, c * 128:(c + 1) * 128],
                           tp2[:].rearrange("p (kc m) -> p kc m", kc=8)))(),
                       reads=[tp2b], writes=[xfmb], tag=f"{tagp}tpev{t}_{c}")
            if t == 0 and _os.environ.get("KDUMPFF"):
                p.emit("gpsimd", lambda e: e.dma_start(out=ff_d[0], in_=xln[:]), reads=[xlnb], dma=True, tag="dump_xln")
                p.emit("gpsimd", lambda e: e.dma_start(out=ff_d[1], in_=xfm[:]), reads=[xfmb], dma=True, tag="dump_xfm")
            for hcx in range(33):
                if hcx < 32:
                    hc = hcx
                    hpi = hc % 2
                    for kc in range(8):
                        p.emit("tensor",
                               (lambda hpi=hpi, kc=kc, hc=hc: lambda e: e.matmul(
                                   hp[hpi][:, 0:TILE],
                                   bw1t[:, kc * HD + hc * 128:kc * HD + (hc + 1) * 128],
                                   xfm[:, kc * TILE:(kc + 1) * TILE],
                                   start=(kc == 0), stop=(kc == 7)))(),
                               reads=[bw1b, xfmb], writes=[hpb[hpi]], tag=f"{tagp}hmm{t}_{hc}_{kc}")
                    p.emit("scalar",
                           (lambda hpi=hpi, hc=hc: lambda e: e.activation(
                               hbf[hpi][:], hp[hpi][:, 0:TILE], AF.Gelu, bias=b1_cols[:, hc:hc + 1], scale=1.0))(),
                           reads=[hpb[hpi], bias_bf], writes=[hbfb[hpi]], tag=f"{tagp}gelu{t}_{hc}")
                if hcx >= 1:
                    hc = hcx - 1
                    hpi = hc % 2
                    for c in range(2):
                        for nh in range(2):
                            p.emit("tensor",
                                   (lambda c=c, nh=nh, hc=hc, hpi=hpi: lambda e: e.matmul(
                                       op_ps[c][:, nh * 512:(nh + 1) * 512],
                                       hbf[hpi][:, c * 128:(c + 1) * 128],
                                       bw2t[:, hc * D + nh * 512:hc * D + (nh + 1) * 512],
                                       start=(hc == 0), stop=False))(),
                                   reads=[hbfb[hpi], bw2b], writes=[opb[c]], tag=f"{tagp}omm{t}_{hc}_{c}_{nh}")
            for c in range(2):
                for nh in range(2):
                    p.emit("tensor",
                           (lambda c=c, nh=nh: lambda e: e.matmul(
                               op_ps[c][:, nh * 512:(nh + 1) * 512],
                               ones1_bf[:], b2_bf[:, nh * 512:(nh + 1) * 512],
                               start=False, stop=True))(),
                           reads=[bias_bf, consts], writes=[opb[c]], tag=f"{tagp}ob{t}_{c}_{nh}")
                p.emit("vector",
                       (lambda c=c, ot=ot, xt=xt: lambda e: e.scalar_tensor_tensor(
                           ot[:, c * D:(c + 1) * D], op_ps[c][:], 0.5, xt[:, c * D:(c + 1) * D],
                           ALU.mult, ALU.add))(),
                       reads=[opb[c], xb], writes=[ob], tag=f"{tagp}res{t}_{c}")
            p.emit("gpsimd",
                   (lambda ot=ot, tok0=tok0: lambda e: e.dma_start(
                       out=dst_d[tok0:tok0 + TILE, :].rearrange("(c p) f -> p c f", c=2), in_=ot[:].rearrange("p (c f) -> p c f", c=2)))(),
                   reads=[ob], dma=True, tag=f"{tagp}st{t}")
            if _os.environ.get("KBAR"):
                p.barrier()

    if True:
      bw1t = bfv(0, 8 * HD)
      bw2t = bfv(8 * HD, 32 * D)
      bw1b, bw2b = Buf(), Buf()
      if True:
        xst = [f32v(0, D), f32v(D, D), f32v(2 * D, D)]
        xstb = [Buf() for _ in range(3)]
        scr_bf = bfv(65536, D)
        scrb = Buf()
        s1g = f32v(5120, n_cs)
        s2g = f32v(5184, n_cs)
        rawb = Buf()
        for tc in range(n_cs):
            bi = tc % 3
            xb, xt = xstb[bi], xst[bi]
            p.emit("sync",
                   (lambda xt=xt, tc=tc: lambda e: e.dma_start(out=xt[:], in_=x_d[tc * 128:(tc + 1) * 128, :]))(),
                   writes=[xb], dma=True, tag=f"p0ld{tc}")
            p.emit("vector",
                   (lambda xt=xt, tc=tc: lambda e: e.tensor_scalar(
                       scr_bf[:], xt[:], 1.0, 0.0, ALU.mult, ALU.add, accum_out=s1g[:, tc:tc + 1]))(),
                   reads=[xb], writes=[scrb, rawb], tag=f"p0s1_{tc}")
            p.emit("vector",
                   (lambda xt=xt, tc=tc: lambda e: e.scalar_tensor_tensor(
                       scr_bf[:], xt[:], 0.0, xt[:], ALU.add, ALU.mult, accum_out=s2g[:, tc:tc + 1]))(),
                   reads=[xb], writes=[scrb, rawb], tag=f"p0s2_{tc}")
        emit_stat_finish(s1g, s2g, rawb, m1_sb, r1_sb, stats1, n_cs)
        if _os.environ.get("KDUMPST"):
            p.emit("gpsimd", lambda e: e.dma_start(out=st_d[:, 0:n_cs], in_=m1_sb[:, 0:n_cs]), reads=[stats1], dma=True, tag="dump_m1")
            p.emit("gpsimd", lambda e: e.dma_start(out=st_d[:, 64:64 + n_cs], in_=r1_sb[:, 0:n_cs]), reads=[stats1], dma=True, tag="dump_r1")
            p.emit("gpsimd", lambda e: e.dma_start(out=st_d[:, 128:128 + n_cs], in_=s1g[:, 0:n_cs]), reads=[rawb], dma=True, tag="dump_s1")
            p.emit("gpsimd", lambda e: e.dma_start(out=st_d[:, 192:192 + n_cs], in_=s2g[:, 0:n_cs]), reads=[rawb], dma=True, tag="dump_s2")

        b1row = f32v(8192, HD, rows=1)
        b1rowb = Buf()
        p.emit("sync", lambda e: e.dma_start(out=b1row[:], in_=b1_d[None, :]), writes=[b1rowb], dma=True, tag="b1ld")
        for hc in range(32):
            p.emit("tensor",
                   (lambda hc=hc: lambda e: e.transpose(
                       ptiny[:, hc:hc + 1], b1row[:, hc * 128:(hc + 1) * 128], ones1_f[0:1, 0:1]))(),
                   reads=[b1rowb, consts], writes=[tinyb0], tag=f"b1tp{hc}")
        p.emit("vector", lambda e: e.tensor_copy(b1_cols[:], ptiny[:, 0:32]),
               reads=[tinyb0], writes=[bias_bf], tag="b1ev")
        wst = [f32v(3072, D), f32v(4096, D)]
        wstb = [Buf() for _ in range(2)]
        sgn = bfv(66560, D)
        sgnb = Buf()
        bwc = bfv(67584, D)
        bwcb = Buf()
        acol = f32v(5264, 1)
        acolb = Buf()
        acc4 = f32v(5280, 4)
        acc4b = Buf()
        tp_ps = ptp
        tpb = Buf()

        for rc in range(32):
            bi = rc % 2
            wt, wb = wst[bi], wstb[bi]
            p.emit("sync",
                   (lambda wt=wt, rc=rc: lambda e: e.dma_start(out=wt[:], in_=w1_d[rc * 128:(rc + 1) * 128, :]))(),
                   writes=[wb], dma=True, tag=f"w1ld{rc}")
            p.emit("vector",
                   (lambda wt=wt: lambda e: e.reduce_sum(acol[:], wt[:], AX.X, apply_absolute_value=True))(),
                   reads=[wb], writes=[acolb], tag=f"w1abs{rc}")
            p.emit("scalar",
                   (lambda wt=wt: lambda e: e.activation(sgn[:], wt[:], AF.Sign))(),
                   reads=[wb], writes=[sgnb], tag=f"w1sgn{rc}")
            p.emit("vector",
                   lambda e: e.tensor_scalar(bwc[:], sgn[:], acol[:], 1.0 / D, ALU.mult, ALU.mult),
                   reads=[sgnb, acolb], writes=[bwcb], tag=f"w1bw{rc}")
            for kc in range(8):
                p.emit("tensor",
                       (lambda kc=kc: lambda e: e.transpose(
                           tp_ps[:, kc * 128:(kc + 1) * 128], bwc[:, kc * 128:(kc + 1) * 128], ident_bf[:]))(),
                       reads=[bwcb, consts], writes=[tpb], tag=f"w1tp{rc}_{kc}")
            p.emit("scalar",
                   (lambda rc=rc: lambda e: e.copy(
                       bw1t[:].rearrange("p (kc m) -> p kc m", kc=8)[:, :, rc * 128:(rc + 1) * 128],
                       tp_ps[:].rearrange("p (kc m) -> p kc m", kc=8)))(),
                   reads=[tpb], writes=[bw1b], tag=f"w1ev{rc}")
        for rc in range(8):
            for qtr in range(4):
                wt, wb = wst[qtr % 2], wstb[qtr % 2]
                p.emit("sync",
                       (lambda wt=wt, rc=rc, q=qtr: lambda e: e.dma_start(
                           out=wt[:], in_=w2_d[rc * 128:(rc + 1) * 128, q * 1024:(q + 1) * 1024]))(),
                       writes=[wb], dma=True, tag=f"w2ld{rc}_{qtr}")
                p.emit("vector",
                       (lambda wt=wt, q=qtr: lambda e: e.reduce_sum(acc4[:, q:q + 1], wt[:], AX.X, apply_absolute_value=True))(),
                       reads=[wb], writes=[acc4b], tag=f"w2abs{rc}_{qtr}")
            p.emit("vector",
                   lambda e: e.reduce_sum(acol[:], acc4[:], AX.X),
                   reads=[acc4b], writes=[acolb], tag=f"w2al{rc}")
            for qtr in range(4):
                wt, wb = wst[qtr % 2], wstb[qtr % 2]
                p.emit("sync",
                       (lambda wt=wt, rc=rc, q=qtr: lambda e: e.dma_start(
                           out=wt[:], in_=w2_d[rc * 128:(rc + 1) * 128, q * 1024:(q + 1) * 1024]))(),
                       writes=[wb], dma=True, tag=f"w2ld2{rc}_{qtr}")
                p.emit("scalar",
                       (lambda wt=wt: lambda e: e.activation(sgn[:], wt[:], AF.Sign))(),
                       reads=[wb], writes=[sgnb], tag=f"w2sgn{rc}_{qtr}")
                p.emit("vector",
                       lambda e: e.tensor_scalar(bwc[:], sgn[:], acol[:], 1.0 / HD, ALU.mult, ALU.mult),
                       reads=[sgnb, acolb], writes=[bwcb], tag=f"w2bw{rc}_{qtr}")
                for hc8 in range(8):
                    p.emit("tensor",
                           (lambda hc8=hc8: lambda e: e.transpose(
                               tp_ps[:, hc8 * 128:(hc8 + 1) * 128], bwc[:, hc8 * 128:(hc8 + 1) * 128], ident_bf[:]))(),
                           reads=[bwcb, consts], writes=[tpb], tag=f"w2tp{rc}_{qtr}_{hc8}")
                p.emit("scalar",
                       (lambda rc=rc, q=qtr: lambda e: e.copy(
                           bw2t[:].rearrange("p (hc m) -> p hc m", hc=32)[:, q * 8:(q + 1) * 8, rc * 128:(rc + 1) * 128],
                           tp_ps[:].rearrange("p (hc m) -> p hc m", hc=8)))(),
                       reads=[tpb], writes=[bw2b], tag=f"w2ev{rc}_{qtr}")
        for kc in range(8):
            p.emit("gpsimd",
                   (lambda kc=kc: lambda e: e.dma_start(
                       out=bw1t_d[:, kc * HD:(kc + 1) * HD], in_=bw1t[:, kc * HD:(kc + 1) * HD]))(),
                   reads=[bw1b], dma=True, tag=f"bw1st{kc}")
            p.emit("gpsimd",
                   (lambda kc=kc: lambda e: e.dma_start(
                       out=bw2t_d[:, kc * 4 * D:(kc + 1) * 4 * D], in_=bw2t[:, kc * 4 * D:(kc + 1) * 4 * D]))(),
                   reads=[bw2b], dma=True, tag=f"bw2st{kc}")

      p.barrier()
      if "A" in phases:
          emit_ffn_phase(x_d, x1_d, m1_sb, r1_sb, stats1, bw1t, bw1b, bw2t, bw2b, "fa")
    p.barrier()

    if "B" in phases:
        Wq_sb = bfv(0, 8 * D)
        Wk_sb = bfv(8 * D, 8 * D)
        Wv_sb = bfv(16 * D, 8 * D)
        Wg_sb = bfv(24 * D, 8 * D)
        Wo_sb = bfv(32 * D, 8 * D)
        Wgt_sb = bfv(40 * D, 8 * NH)
        cw_sb = bfv(41088, 4 * D + NH, rows=1)
        wqb, wkb, wvb, wgb, wob, wgtb, cwb = (Buf() for _ in range(7))
        wmap = [(wq_d, Wq_sb, wqb), (wk_d, Wk_sb, wkb), (wv_d, Wv_sb, wvb),
                (wg_d, Wg_sb, wgb), (wo_d, Wo_sb, wob)]
        for wd, wsb_, wb_ in wmap:
            for kc in range(8):
                p.emit("gpsimd",
                       (lambda wd=wd, wsb_=wsb_, kc=kc: lambda e: e.dma_start(
                           out=wsb_[:, kc * D:(kc + 1) * D], in_=wd[kc * 128:(kc + 1) * 128, :]))(),
                       writes=[wb_], dma=True, tag="wld")
        for kc in range(8):
            p.emit("gpsimd",
                   (lambda kc=kc: lambda e: e.dma_start(
                       out=Wgt_sb[:, kc * NH:(kc + 1) * NH], in_=wgt_d[kc * 128:(kc + 1) * 128, :]))(),
                   writes=[wgtb], dma=True, tag="wgtld")

        ps_tiny = ptiny
        ps_tp = ptp
        psQG = pbig0
        psKAT = pbig1
        psO1 = phalf0
        psKV = phalf1
        tinyb, tpb2, psQGb, psO1b, psKVb = (Buf() for _ in range(5))
        psKATbH = [Buf(), Buf()]

        for wi, (_, wsb_, wb_) in enumerate(wmap[:4]):
            for half in range(2):
                for kc in range(8):
                    p.emit("tensor",
                           (lambda wsb_=wsb_, kc=kc, half=half: lambda e: e.matmul(
                               ps_tiny[0:1, 0:512], onesP_bf[:],
                               wsb_[:, kc * D + half * 512:kc * D + (half + 1) * 512],
                               start=(kc == 0), stop=(kc == 7)))(),
                           reads=[wb_, consts], writes=[tinyb], tag="cwmm")
                p.emit("vector",
                       (lambda wi=wi, half=half: lambda e: e.tensor_copy(
                           cw_sb[:, wi * D + half * 512:wi * D + (half + 1) * 512], ps_tiny[0:1, 0:512]))(),
                       reads=[tinyb], writes=[cwb], tag="cwev")
        for kc in range(8):
            p.emit("tensor",
                   (lambda kc=kc: lambda e: e.matmul(
                       ps_tiny[0:1, 0:NH], onesP_bf[:], Wgt_sb[:, kc * NH:(kc + 1) * NH],
                       start=(kc == 0), stop=(kc == 7)))(),
                   reads=[wgtb, consts], writes=[tinyb], tag="cwgtmm")
        p.emit("vector",
               lambda e: e.tensor_copy(cw_sb[:, 4 * D:4 * D + NH], ps_tiny[0:1, 0:NH]),
               reads=[tinyb], writes=[cwb], tag="cwgtev")

        x1sb = [f32v(0, 2048), f32v(2048, 2048)]
        x1sbb = [Buf() for _ in range(2)]
        xbf_tm = bfv(45200, 2048)
        xbftmb = Buf()
        xbf_fmB = [bfv(47248, 2048), bfv(69008, 2048)]
        xbffmbB = [Buf(), Buf()]
        sqscr = bfv(49296, D)
        sqb = Buf()
        sqscr2 = bfv(71312, D)
        sq2b = Buf()
        o_sb = f32v(12544, D)
        osbb = Buf()
        qd_tm = bfv(50320, 2048)
        qdtmb = Buf()
        kd_tm = bfv(52368, 2048)
        kdtmb = Buf()
        qd_fm = bfv(54416, 4096, rows=64)
        qdfmb = Buf()
        kd_fm = bfv(58512, 4096, rows=64)
        kdfmb = Buf()
        v_bf = bfv(62608, 2048)
        vbfb = Buf()
        g_sb = f32v(8192, D)
        gb = Buf()
        gscr = f32v(9216, D)
        gscrb = Buf()
        gscr2 = f32v(10240, D)
        gscr2b = Buf()
        gsil = bfv(64656, D)
        gsilb = Buf()
        og_tm = bfv(65680, D)
        ogtmb = Buf()
        og_fm = bfv(66704, D)
        ogfmb = Buf()
        A_sb = bfv(67728, 1024)
        Ab = [Buf() for _ in range(8)]
        oscr = f32v(11264, D)
        oscrb = Buf()
        x2o = [f32v(4096, 2048), f32v(6144, 2048)]
        x2ob = [Buf() for _ in range(2)]
        nmrowP = [bfv(68752, TILE, rows=1), bfv(71056, TILE, rows=1)]
        nmrowbP = [Buf(), Buf()]
        fac_sbP = [[f32v(12304, 8), f32v(12512, 8)], [f32v(12352, 8), f32v(12400, 8)]]
        facbP = [[Buf(), Buf()], [Buf(), Buf()]]
        stt_sbP = [[f32v(12288, 4), f32v(12520, 4)], [f32v(12360, 4), f32v(12408, 4)]]
        sttbP = [[Buf(), Buf()], [Buf(), Buf()]]
        gt_sb = f32v(12320, NH)
        gtb = Buf()
        gt2_sb = f32v(12336, NH)
        gt2b = Buf()
        E1_sb = f32v(12368, NH)
        E1b = Buf()
        E2_sb = f32v(12384, NH)
        E2b = Buf()
        eBexp_sb = f32v(12416, NH)
        eB2b = Buf()
        Braw_sb = f32v(12496, NH, rows=1)
        Brawb = Buf()
        oss_sb = f32v(12432, NH)
        ossb = Buf()
        rfac_sb = f32v(12448, NH)
        rfacb = Buf()

        def chunk_stats(t, c, xt, xb, xbf_fm, xbffmb, fac_sb, facb, stt_sb, sttb, nmrow, nmrowb):
            p.emit("vector",
                   (lambda xt=xt, c=c: lambda e: e.tensor_scalar(
                       xbf_tm[:, c * D:(c + 1) * D], xt[:, c * D:(c + 1) * D], 1.0, 0.0,
                       ALU.mult, ALU.add, accum_out=fac_sb[:, 7:8]))(),
                   reads=[xb], writes=[xbftmb, facb], tag=f"bc1_{t}_{c}")
            p.emit("vector",
                   (lambda xt=xt, c=c: lambda e: e.scalar_tensor_tensor(
                       sqscr2[:], xt[:, c * D:(c + 1) * D], 0.0, xt[:, c * D:(c + 1) * D],
                       ALU.add, ALU.mult, accum_out=fac_sb[:, 6:7]))(),
                   reads=[xb], writes=[sq2b, facb], tag=f"bc2_{t}_{c}")
            p.emit("vector",
                   lambda e: e.tensor_scalar(fac_sb[:, 5:6], fac_sb[:, 7:8], 1.0 / D, None, ALU.mult),
                   reads=[facb], writes=[facb], tag=f"bm_{t}_{c}")
            p.emit("vector",
                   lambda e: e.tensor_tensor(stt_sb[:, 0:1], fac_sb[:, 5:6], fac_sb[:, 5:6], ALU.mult),
                   reads=[facb], writes=[sttb], tag=f"bm2_{t}_{c}")
            p.emit("vector",
                   lambda e: e.scalar_tensor_tensor(fac_sb[:, 4:5], fac_sb[:, 6:7], 1.0 / D, stt_sb[:, 0:1], ALU.mult, ALU.subtract),
                   reads=[facb, sttb], writes=[facb], tag=f"bvar_{t}_{c}")
            p.emit("scalar",
                   lambda e: e.activation(stt_sb[:, 1:2], fac_sb[:, 4:5], AF.Ln, bias=float(LN_EPS), scale=1.0),
                   reads=[facb], writes=[sttb], tag=f"blnv_{t}_{c}")
            p.emit("scalar",
                   lambda e: e.activation(fac_sb[:, 3:4], stt_sb[:, 1:2], AF.Exp, bias=0.0, scale=-0.5),
                   reads=[sttb], writes=[facb], tag=f"binv_{t}_{c}")
            p.emit("vector",
                   lambda e: e.tensor_scalar(fac_sb[:, 2:3], fac_sb[:, 4:5], float(RMS_EPS), float(RMS_EPS * LN_EPS), ALU.mult, ALU.add),
                   reads=[facb], writes=[facb], tag=f"bs2e_{t}_{c}")
            if blevel < 2:
                return
            for kc in range(8):
                p.emit("tensor",
                       (lambda c=c, kc=kc: lambda e: e.transpose(
                           ps_tp[:, kc * 128:(kc + 1) * 128],
                           xbf_tm[:, c * D + kc * 128:c * D + (kc + 1) * 128], ident_bf[:]))(),
                       reads=[xbftmb, consts], writes=[tpb2], tag=f"bxtp_{t}_{c}_{kc}")
            p.emit("scalar",
                   (lambda c=c: lambda e: e.copy(
                       xbf_fm[:].rearrange("p (kc tt) -> p kc tt", kc=8)[:, :, c * 128:(c + 1) * 128],
                       ps_tp[:].rearrange("p (kc m) -> p kc m", kc=8)))(),
                   reads=[tpb2], writes=[xbffmb], tag=f"bxtpe_{t}_{c}")
            for kc in range(8):
                p.emit("tensor",
                       (lambda c=c, kc=kc, xbf_fm=xbf_fm: lambda e: e.matmul(
                           ps_tiny[0:1, 384:512], onesP_bf[:],
                           xbf_fm[:, kc * TILE + c * 128:kc * TILE + (c + 1) * 128],
                           start=(kc == 0), stop=(kc == 7)))(),
                       reads=[xbffmb, consts], writes=[tinyb], tag=f"bnms_{t}_{c}_{kc}")
            p.emit("vector",
                   (lambda c=c, nmrow=nmrow: lambda e: e.tensor_scalar(
                       nmrow[:, c * 128:(c + 1) * 128], ps_tiny[0:1, 384:512], -1.0 / D, None, ALU.mult))(),
                   reads=[tinyb], writes=[nmrowb], tag=f"bnme_{t}_{c}")

        def mkck(t, c):
            bi = t % 2
            return dict(t=t, c=c, xt=x1sb[bi], xb=x1sbb[bi], x2t=x2o[bi], x2b=x2ob[bi],
                        xbf_fm=xbf_fmB[bi], xbffmb=xbffmbB[bi],
                        fac_sb=fac_sbP[bi][c], facb=facbP[bi][c],
                        stt_sb=stt_sbP[bi][c], sttb=sttbP[bi][c],
                        nmrow=nmrowP[bi], nmrowb=nmrowbP[bi])

        def emit_kproj(ck):
            t, c = ck["t"], ck["c"]
            xbf_fm, xbffmb, nmrow, nmrowb = ck["xbf_fm"], ck["xbffmb"], ck["nmrow"], ck["nmrowb"]
            for kc in range(8):
                for half in range(2):
                    p.emit("tensor",
                           (lambda c=c, kc=kc, half=half, xbf_fm=xbf_fm: lambda e: e.matmul(
                               psKAT[:, half * 512:(half + 1) * 512],
                               xbf_fm[:, kc * TILE + c * 128:kc * TILE + (c + 1) * 128],
                               Wk_sb[:, kc * D + half * 512:kc * D + (half + 1) * 512],
                               start=(kc == 0), stop=False))(),
                           reads=[xbffmb, wkb], writes=[psKATbH[half]], tag=f"bk_{t}_{c}_{kc}_{half}")
            for half in range(2):
                p.emit("tensor",
                       (lambda c=c, half=half, nmrow=nmrow: lambda e: e.matmul(
                           psKAT[:, half * 512:(half + 1) * 512], nmrow[:, c * 128:(c + 1) * 128],
                           cw_sb[:, D + half * 512:D + (half + 1) * 512], start=False, stop=True))(),
                       reads=[nmrowb, cwb], writes=[psKATbH[half]], tag=f"bkc_{t}_{c}_{half}")

        def _emit_gtladder(t, c, fac_sb, facb):
            p.emit("vector",
                   (lambda fac_sb=fac_sb: lambda e: e.tensor_scalar(gt_sb[:], ps_tiny[:, 0:NH], fac_sb[:, 3:4], None, ALU.mult))(),
                   reads=[tinyb, facb], writes=[gtb], tag=f"bze_{t}_{c}")
            p.emit("scalar", lambda e: e.activation(gt2_sb[:], gt_sb[:], AF.Abs),
                   reads=[gtb], writes=[gt2b], tag=f"bga_{t}_{c}")
            p.emit("scalar", lambda e: e.activation(gt2_sb[:], gt2_sb[:], AF.Exp, bias=0.0, scale=-1.0),
                   reads=[gt2b], writes=[gt2b], tag=f"bgxe_{t}_{c}")
            p.emit("scalar", lambda e: e.activation(gt2_sb[:], gt2_sb[:], AF.Ln, bias=1.0, scale=1.0),
                   reads=[gt2b], writes=[gt2b], tag=f"bgl_{t}_{c}")
            p.emit("vector",
                   lambda e: e.scalar_tensor_tensor(gt_sb[:], gt_sb[:], 0.0, gt2_sb[:], ALU.min, ALU.subtract),
                   reads=[gtb, gt2b], writes=[gtb], tag=f"bgs_{t}_{c}")
            if debug and t < 2:
                di = t * 2 + c
                p.emit("gpsimd", (lambda di=di: lambda e: e.dma_start(out=dbg_d[di, :, 0:NH], in_=gt_sb[:]))(),
                       reads=[gtb], dma=True, tag=f"dbggt{t}_{c}")

        def _emit_vproj(ck, half):
            t, c = ck["t"], ck["c"]
            xbf_fm, xbffmb = ck["xbf_fm"], ck["xbffmb"]
            fac_sb, facb = ck["fac_sb"], ck["facb"]
            stt_sb, sttb = ck["stt_sb"], ck["sttb"]
            nmrow, nmrowb = ck["nmrow"], ck["nmrowb"]
            for kc in range(8):
                p.emit("tensor",
                       (lambda c=c, kc=kc, half=half, xbf_fm=xbf_fm: lambda e: e.matmul(
                           ps_tiny[:, 0:512],
                           xbf_fm[:, kc * TILE + c * 128:kc * TILE + (c + 1) * 128],
                           Wv_sb[:, kc * D + half * 512:kc * D + (half + 1) * 512],
                           start=(kc == 0), stop=False))(),
                       reads=[xbffmb, wvb], writes=[tinyb], tag=f"bv_{t}_{c}_{half}_{kc}")
            p.emit("tensor",
                   (lambda c=c, half=half, nmrow=nmrow: lambda e: e.matmul(
                       ps_tiny[:, 0:512], nmrow[:, c * 128:(c + 1) * 128],
                       cw_sb[:, 2 * D + half * 512:2 * D + (half + 1) * 512], start=False, stop=True))(),
                   reads=[nmrowb, cwb], writes=[tinyb], tag=f"bvc_{t}_{c}_{half}")
            p.emit("scalar",
                   (lambda half=half, stt_sb=stt_sb: lambda e: e.activation(
                       sqscr[:, 0:512], ps_tiny[:, 0:512], AF.Square, accum_out=stt_sb[:, 2 + half:3 + half]))(),
                   reads=[tinyb], writes=[sqb, sttb], tag=f"bvss_{t}_{c}_{half}")
            p.emit("scalar",
                   (lambda c=c, half=half: lambda e: e.activation(
                       v_bf[:, c * D + half * 512:c * D + (half + 1) * 512], ps_tiny[:, 0:512], AF.Copy))(),
                   reads=[tinyb], writes=[vbfb], tag=f"bve_{t}_{c}_{half}")
            if half == 0:
                return
            p.emit("vector",
                   lambda e: e.tensor_add(stt_sb[:, 0:1], stt_sb[:, 2:3], stt_sb[:, 3:4]),
                   reads=[sttb], writes=[sttb], tag=f"bvsa_{t}_{c}")
            p.emit("vector",
                   lambda e: e.scalar_tensor_tensor(stt_sb[:, 1:2], stt_sb[:, 0:1], 1.0 / D, fac_sb[:, 2:3], ALU.mult, ALU.add),
                   reads=[sttb, facb], writes=[sttb], tag=f"bvf1_{t}_{c}")
            p.emit("scalar",
                   lambda e: e.activation(stt_sb[:, 1:2], stt_sb[:, 1:2], AF.Ln),
                   reads=[sttb], writes=[sttb], tag=f"bvf2_{t}_{c}")
            p.emit("scalar",
                   lambda e: e.activation(fac_sb[:, 4:5], stt_sb[:, 1:2], AF.Exp, bias=0.0, scale=-0.5),
                   reads=[sttb], writes=[facb], tag=f"bvf3_{t}_{c}")
            if debug and t < 2:
                di = t * 2 + c
                p.emit("gpsimd", (lambda di=di, c=c: lambda e: e.dma_start(out=dbg_bf[di + 8, :, 0:D], in_=v_bf[:, c * D:(c + 1) * D]))(),
                       reads=[vbfb], dma=True, tag=f"dbgv{t}_{c}")

        def _emit_zgt(ck):
            t, c = ck["t"], ck["c"]
            xbf_fm, xbffmb = ck["xbf_fm"], ck["xbffmb"]
            fac_sb, facb = ck["fac_sb"], ck["facb"]
            nmrow, nmrowb = ck["nmrow"], ck["nmrowb"]
            for kc in range(8):
                p.emit("tensor",
                       (lambda c=c, kc=kc, xbf_fm=xbf_fm: lambda e: e.matmul(
                           ps_tiny[:, 0:NH],
                           xbf_fm[:, kc * TILE + c * 128:kc * TILE + (c + 1) * 128],
                           Wgt_sb[:, kc * NH:(kc + 1) * NH],
                           start=(kc == 0), stop=False))(),
                       reads=[xbffmb, wgtb], writes=[tinyb], tag=f"bz_{t}_{c}_{kc}")
            p.emit("tensor",
                   (lambda c=c, nmrow=nmrow: lambda e: e.matmul(
                       ps_tiny[:, 0:NH], nmrow[:, c * 128:(c + 1) * 128],
                       cw_sb[:, 4 * D:4 * D + NH], start=False, stop=True))(),
                   reads=[nmrowb, cwb], writes=[tinyb], tag=f"bzc_{t}_{c}")
            _emit_gtladder(t, c, fac_sb, facb)

        def proj_block(ck, only_q=False):
            t, c = ck["t"], ck["c"]
            xbf_fm, xbffmb = ck["xbf_fm"], ck["xbffmb"]
            nmrow, nmrowb = ck["nmrow"], ck["nmrowb"]
            if not only_q:
                _emit_zgt(ck)
            for kc in range(8):
                for half in range(2):
                    p.emit("tensor",
                           (lambda c=c, kc=kc, half=half, xbf_fm=xbf_fm: lambda e: e.matmul(
                               psQG[:, half * 512:(half + 1) * 512],
                               xbf_fm[:, kc * TILE + c * 128:kc * TILE + (c + 1) * 128],
                               Wq_sb[:, kc * D + half * 512:kc * D + (half + 1) * 512],
                               start=(kc == 0), stop=False))(),
                           reads=[xbffmb, wqb], writes=[psQGb], tag=f"bq_{t}_{c}_{kc}_{half}")
            for half in range(2):
                p.emit("tensor",
                       (lambda c=c, half=half, nmrow=nmrow: lambda e: e.matmul(
                           psQG[:, half * 512:(half + 1) * 512], nmrow[:, c * 128:(c + 1) * 128],
                           cw_sb[:, half * 512:(half + 1) * 512], start=False, stop=True))(),
                       reads=[nmrowb, cwb], writes=[psQGb], tag=f"bqc_{t}_{c}_{half}")
            if only_q:
                return
            emit_kproj(ck)
            _emit_vproj(ck, 0)
            _emit_vproj(ck, 1)

        def chunk_compute(ck, nk):
            t, c = ck["t"], ck["c"]
            xt, xb, x2t, x2b = ck["xt"], ck["xb"], ck["x2t"], ck["x2b"]
            xbf_fm, xbffmb = ck["xbf_fm"], ck["xbffmb"]
            fac_sb, facb = ck["fac_sb"], ck["facb"]
            stt_sb, sttb = ck["stt_sb"], ck["sttb"]
            tc = t * 2 + c
            if tc % (tiles_per_seq * 2) == 0:
                p.emit("vector", lambda e: e.memset(S_sb[:], 0.0), reads=[Sbuf], writes=[Sbuf], tag=f"bS0_{tc}")
                p.emit("vector", lambda e: e.memset(S_bf[:], 0.0), reads=[Sbuf], writes=[Sbuf], tag=f"bS0b_{tc}")
            if debug and t < 2:
                di = t * 2 + c
                p.emit("gpsimd", (lambda di=di, fac_sb=fac_sb: lambda e: e.dma_start(out=dbg_d[di, :, 6 * NH:6 * NH + 8], in_=fac_sb[:]))(),
                       reads=[facb], dma=True, tag=f"dbgfac_{t}_{c}")
            p.emit("scalar",
                   lambda e: e.activation(sqscr[:], psQG[:], AF.Square, accum_out=stt_sb[:, 0:1]),
                   reads=[psQGb], writes=[sqb, sttb], tag=f"bqss_{t}_{c}")
            p.emit("tensor",
                   lambda e: e.matmul(ps_tiny[:, 32:48], cumu_f[:], gt_sb[:], start=True, stop=True),
                   reads=[gtb, consts], writes=[tinyb], tag=f"bcm_{t}_{c}")
            p.emit("tensor",
                   lambda e: e.matmul(ps_tiny[0:1, 48:64], onesP16_f[:], gt_sb[:], start=True, stop=True),
                   reads=[gtb, consts], writes=[tinyb], tag=f"bB_{t}_{c}")
            p.emit("scalar", lambda e: e.activation(E1_sb[:], ps_tiny[:, 32:48], AF.Exp),
                   reads=[tinyb], writes=[E1b], tag=f"bE1_{t}_{c}")
            p.emit("scalar", lambda e: e.activation(E2_sb[:], ps_tiny[:, 32:48], AF.Exp, bias=0.0, scale=-1.0),
                   reads=[tinyb], writes=[E2b], tag=f"bE2e_{t}_{c}")
            p.emit("vector", lambda e: e.tensor_copy(Braw_sb[:], ps_tiny[0:1, 48:64]),
                   reads=[tinyb], writes=[Brawb], tag=f"bBr_{t}_{c}")
            p.emit("tensor",
                   lambda e: e.matmul(ps_tiny[:, 64:80], ones1_f[:], Braw_sb[:], start=True, stop=True),
                   reads=[Brawb, consts], writes=[tinyb], tag=f"beBb_{t}_{c}")
            p.emit("vector", lambda e: e.tensor_copy(eBbc_sb[:], ps_tiny[:, 64:80]),
                   reads=[tinyb], writes=[eB2b], tag=f"beBbe_{t}_{c}")
            p.emit("scalar", lambda e: e.activation(eBexp_sb[:], eBbc_sb[:], AF.Exp),
                   reads=[eB2b], writes=[eB2b], tag=f"beB2x_{t}_{c}")
            if blevel < 5:
                return
            p.emit("vector",
                   lambda e: e.scalar_tensor_tensor(stt_sb[:, 1:2], stt_sb[:, 0:1], 1.0 / D, fac_sb[:, 2:3], ALU.mult, ALU.add),
                   reads=[sttb, facb], writes=[sttb], tag=f"bqf1_{t}_{c}")
            p.emit("scalar",
                   lambda e: e.activation(stt_sb[:, 1:2], stt_sb[:, 1:2], AF.Ln),
                   reads=[sttb], writes=[sttb], tag=f"bqf2_{t}_{c}")
            p.emit("scalar",
                   lambda e: e.activation(fac_sb[:, 0:1], stt_sb[:, 1:2], AF.Exp, bias=float(LOG_QSCALE), scale=-0.5),
                   reads=[sttb], writes=[facb], tag=f"bqf3_{t}_{c}")
            p.emit("vector",
                   (lambda c=c: lambda e: e.scalar_tensor_tensor(
                       qd_tm[:, c * D:(c + 1) * D].rearrange("p (h dh) -> p h dh", h=NH),
                       psQG[:].rearrange("p (h dh) -> p h dh", h=NH),
                       fac_sb[:, 0:1], bcast_in1(E1_sb[:][:, :, None], psQG[:].rearrange("p (h dh) -> p h dh", h=NH)),
                       ALU.mult, ALU.mult))(),
                   reads=[psQGb, facb, E1b], writes=[qdtmb], tag=f"bqd_{t}_{c}")
            p.emit("scalar",
                   lambda e: e.activation(sqscr[:], psKAT[:], AF.Square, accum_out=stt_sb[:, 0:1]),
                   reads=[psKATbH[0], psKATbH[1]], writes=[sqb, sttb], tag=f"bkss_{t}_{c}")
            p.emit("vector",
                   lambda e: e.scalar_tensor_tensor(stt_sb[:, 1:2], stt_sb[:, 0:1], 1.0 / D, fac_sb[:, 2:3], ALU.mult, ALU.add),
                   reads=[sttb, facb], writes=[sttb], tag=f"bkf1_{t}_{c}")
            p.emit("scalar",
                   lambda e: e.activation(stt_sb[:, 1:2], stt_sb[:, 1:2], AF.Ln),
                   reads=[sttb], writes=[sttb], tag=f"bkf2_{t}_{c}")
            p.emit("scalar",
                   lambda e: e.activation(fac_sb[:, 1:2], stt_sb[:, 1:2], AF.Exp, bias=0.0, scale=-0.5),
                   reads=[sttb], writes=[facb], tag=f"bkf3_{t}_{c}")
            p.emit("vector",
                   lambda e: e.tensor_mul(fac_sb[:, 1:2], fac_sb[:, 1:2], fac_sb[:, 4:5]),
                   reads=[facb], writes=[facb], tag=f"bkfv_{t}_{c}")
            p.emit("vector",
                   (lambda c=c: lambda e: e.scalar_tensor_tensor(
                       kd_tm[:, c * D:(c + 1) * D].rearrange("p (h dh) -> p h dh", h=NH),
                       psKAT[:].rearrange("p (h dh) -> p h dh", h=NH),
                       fac_sb[:, 1:2], bcast_in1(E2_sb[:][:, :, None], psKAT[:].rearrange("p (h dh) -> p h dh", h=NH)),
                       ALU.mult, ALU.mult))(),
                   reads=[psKATbH[0], psKATbH[1], facb, E2b], writes=[kdtmb], tag=f"bkd_{t}_{c}")
            if debug and t < 2:
                di = t * 2 + c
                p.emit("gpsimd", (lambda di=di: lambda e: e.dma_start(out=dbg_d[di, :, NH:2 * NH], in_=E1_sb[:]))(),
                       reads=[E1b], dma=True, tag=f"dbgE1{t}_{c}")
                p.emit("gpsimd", (lambda di=di: lambda e: e.dma_start(out=dbg_d[di, :, 2 * NH:3 * NH], in_=E2_sb[:]))(),
                       reads=[E2b], dma=True, tag=f"dbgE2{t}_{c}")
                p.emit("gpsimd", (lambda di=di: lambda e: e.dma_start(out=dbg_d[di, :, 4 * NH:4 * NH + 8], in_=eBexp_sb[:, 0:8]))(),
                       reads=[eB2b], dma=True, tag=f"dbgeB2{t}_{c}")
                p.emit("gpsimd", (lambda di=di: lambda e: e.dma_start(out=dbg_bf[di, :, 0:D], in_=qd_tm[:, 0:D] if di % 2 == 0 else qd_tm[:, D:2 * D]))(),
                       reads=[qdtmb], dma=True, tag=f"dbgqd{t}_{c}")
                p.emit("gpsimd", (lambda di=di, c=c: lambda e: e.dma_start(out=dbg_bf[di + 4, :, 0:D], in_=kd_tm[:, c * D:(c + 1) * D]))(),
                       reads=[kdtmb], dma=True, tag=f"dbgkd{t}_{c}")
                p.emit("gpsimd", (lambda di=di: lambda e: e.dma_start(out=dbg_d[di, 0:64, D:D + 1024], in_=S_sb[0:64, :]))(),
                       reads=[Sbuf], dma=True, tag=f"dbgS{t}_{c}")
            if blevel < 6:
                return
            for gi, (src_tm, srcb, dst_fm, dstb, tg) in enumerate(((qd_tm, qdtmb, qd_fm, qdfmb, "q"), (kd_tm, kdtmb, kd_fm, kdfmb, "k"))):
                for q4 in range(4):
                    pp = (gi * 4 + q4) % 2
                    pbank = ps_tp if pp == 0 else psO1
                    pbb = tpb2 if pp == 0 else psO1b
                    for hh in range(4):
                        h = q4 * 4 + hh
                        if pp == 0:
                            fn = (lambda src_tm=src_tm, c=c, h=h, hh=hh, pbank=pbank: lambda e: e.transpose(
                                pbank[0:64, hh * 128:(hh + 1) * 128],
                                src_tm[:, c * D + h * DH:c * D + (h + 1) * DH], ident_bf[:]))()
                        else:
                            fn = (lambda src_tm=src_tm, c=c, h=h, hh=hh, pbank=pbank: lambda e: e.matmul(
                                pbank[0:64, hh * 128:(hh + 1) * 128],
                                src_tm[:, c * D + h * DH:c * D + (h + 1) * DH], ident_bf[:],
                                start=True, stop=True))()
                        p.emit("tensor", fn, reads=[srcb, consts], writes=[pbb], tag=f"b{tg}t_{t}_{c}_{h}")
                    p.emit("scalar",
                           (lambda dst_fm=dst_fm, c=c, q4=q4, pbank=pbank: lambda e: e.copy(
                               dst_fm[0:64, (c * NH + q4 * 4) * 128:(c * NH + q4 * 4 + 4) * 128],
                               pbank[0:64, 0:512]))(),
                           reads=[pbb], writes=[dstb], tag=f"b{tg}te_{t}_{c}_{q4}")
            if blevel < 7:
                return
            for hh in range(8):
                p.emit("tensor",
                       (lambda c=c, hh=hh: lambda e: e.matmul(
                           psKAT[:, hh * 128:(hh + 1) * 128],
                           kd_fm[0:64, (c * NH + hh) * 128:(c * NH + hh + 1) * 128],
                           qd_fm[0:64, (c * NH + hh) * 128:(c * NH + hh + 1) * 128],
                           start=(hh % 4 == 0), stop=(hh % 4 == 3)))(),
                       reads=[kdfmb, qdfmb], writes=[psKATbH[hh // 4]], tag=f"bAT_{t}_{c}_{hh}")
            for hh in range(8):
                p.emit("vector",
                       (lambda hh=hh: lambda e: e.tensor_tensor(
                           A_sb[:, hh * 128:(hh + 1) * 128],
                           psKAT[:, hh * 128:(hh + 1) * 128],
                           tril_f[:], ALU.mult))(),
                       reads=[psKATbH[hh // 4], consts], writes=[Ab[hh]], tag=f"bAe_{t}_{c}_{hh}")
            for kc in range(8):
                for half in range(2):
                    p.emit("tensor",
                           (lambda c=c, kc=kc, half=half, xbf_fm=xbf_fm: lambda e: e.matmul(
                               psQG[:, half * 512:(half + 1) * 512],
                               xbf_fm[:, kc * TILE + c * 128:kc * TILE + (c + 1) * 128],
                               Wg_sb[:, kc * D + half * 512:kc * D + (half + 1) * 512],
                               start=(kc == 0), stop=False))(),
                           reads=[xbffmb, wgb], writes=[psQGb], tag=f"bgp_{t}_{c}_{kc}_{half}")
            for half in range(2):
                p.emit("tensor",
                       (lambda c=c, half=half, nmrow=ck["nmrow"]: lambda e: e.matmul(
                           psQG[:, half * 512:(half + 1) * 512], nmrow[:, c * 128:(c + 1) * 128],
                           cw_sb[:, 3 * D + half * 512:3 * D + (half + 1) * 512], start=False, stop=True))(),
                       reads=[ck["nmrowb"], cwb], writes=[psQGb], tag=f"bgc_{t}_{c}_{half}")
            p.emit("scalar",
                   lambda e: e.activation(g_sb[:], psQG[:], AF.Copy, scale=fac_sb[:, 3:4]),
                   reads=[psQGb, facb], writes=[gb], tag=f"bge2_{t}_{c}")
            p.emit("scalar", lambda e: e.activation(gscr[:], g_sb[:], AF.Abs),
                   reads=[gb], writes=[gscrb], tag=f"bsa_{t}_{c}")
            p.emit("scalar", lambda e: e.activation(gscr[:], gscr[:], AF.Exp, bias=0.0, scale=-1.0),
                   reads=[gscrb], writes=[gscrb], tag=f"bse_{t}_{c}")
            for hh in range(8):
                p.emit("tensor",
                       (lambda c=c, hh=hh: lambda e: e.matmul(
                           psO1[:, hh * DH:(hh + 1) * DH],
                           A_sb[:, hh * 128:(hh + 1) * 128],
                           v_bf[:, c * D + hh * DH:c * D + (hh + 1) * DH],
                           start=(hh == 0), stop=False))(),
                       reads=[Ab[hh], vbfb], writes=[psO1b], tag=f"boi_{t}_{c}_{hh}")
                p.emit("tensor",
                       (lambda c=c, hh=hh: lambda e: e.matmul(
                           psO1[:, hh * DH:(hh + 1) * DH],
                           qd_fm[0:64, (c * NH + hh) * 128:(c * NH + hh + 1) * 128],
                           S_bf[0:64, hh * DH:(hh + 1) * DH],
                           start=False, stop=(hh == 7)))(),
                       reads=[qdfmb, Sbuf], writes=[psO1b], tag=f"boc_{t}_{c}_{hh}")
            for hh in range(8):
                h = 8 + hh
                p.emit("tensor",
                       (lambda c=c, hh=hh, h=h: lambda e: e.matmul(
                           psKAT[:, hh * 128:(hh + 1) * 128],
                           kd_fm[0:64, (c * NH + h) * 128:(c * NH + h + 1) * 128],
                           qd_fm[0:64, (c * NH + h) * 128:(c * NH + h + 1) * 128],
                           start=(hh % 4 == 0), stop=(hh % 4 == 3)))(),
                       reads=[kdfmb, qdfmb], writes=[psKATbH[hh // 4]], tag=f"bAT_{t}_{c}_{h}")
            p.emit("scalar",
                   lambda e: e.copy(o_sb[:, 0:512], psO1[:, 0:512]),
                   reads=[psO1b], writes=[osbb], tag=f"boev0_{t}_{c}")
            p.emit("scalar",
                   lambda e: e.activation(oscr[:, 0:512], o_sb[:, 0:512], AF.Square),
                   reads=[osbb], writes=[oscrb], tag=f"bosq0_{t}_{c}")
            p.emit("vector",
                   lambda e: e.tensor_reduce(
                       oss_sb[:, 0:8][:, :, None], oscr[:, 0:512].rearrange("p (h dh) -> p h dh", h=8),
                       AX.X, ALU.add),
                   reads=[oscrb], writes=[ossb], tag=f"boss0_{t}_{c}")
            p.emit("scalar", lambda e: e.activation(gscr[:], gscr[:], AF.Ln, bias=1.0, scale=1.0),
                   reads=[gscrb], writes=[gscrb], tag=f"bsl_{t}_{c}")
            for hh in range(8):
                h = 8 + hh
                p.emit("vector",
                       (lambda hh=hh: lambda e: e.tensor_tensor(
                           A_sb[:, hh * 128:(hh + 1) * 128],
                           psKAT[:, hh * 128:(hh + 1) * 128],
                           tril_f[:], ALU.mult))(),
                       reads=[psKATbH[hh // 4], consts], writes=[Ab[hh]], tag=f"bAe_{t}_{c}_{h}")
            p.emit("vector",
                   lambda e: e.scalar_tensor_tensor(gscr2[:], g_sb[:], 0.0, gscr[:], ALU.min, ALU.subtract),
                   reads=[gb, gscrb], writes=[gscr2b], tag=f"bssm_{t}_{c}")
            for hh in range(8):
                p.emit("tensor",
                       (lambda c=c, hh=hh: lambda e: e.matmul(
                           psKV[0:64, hh * DH:(hh + 1) * DH],
                           kd_tm[:, c * D + hh * DH:c * D + (hh + 1) * DH],
                           v_bf[:, c * D + hh * DH:c * D + (hh + 1) * DH],
                           start=(hh == 0), stop=(hh == 7)))(),
                       reads=[kdtmb, vbfb], writes=[psKVb], tag=f"bkv_{t}_{c}_{hh}")
            if nk is not None:
                _emit_vproj(nk, 0)
            for hh in range(8):
                h = 8 + hh
                p.emit("tensor",
                       (lambda c=c, hh=hh, h=h: lambda e: e.matmul(
                           psO1[:, hh * DH:(hh + 1) * DH],
                           A_sb[:, hh * 128:(hh + 1) * 128],
                           v_bf[:, c * D + h * DH:c * D + (h + 1) * DH],
                           start=(hh == 0), stop=False))(),
                       reads=[Ab[hh], vbfb], writes=[psO1b], tag=f"boi_{t}_{c}_{h}")
                p.emit("tensor",
                       (lambda c=c, hh=hh, h=h: lambda e: e.matmul(
                           psO1[:, hh * DH:(hh + 1) * DH],
                           qd_fm[0:64, (c * NH + h) * 128:(c * NH + h + 1) * 128],
                           S_bf[0:64, h * DH:(h + 1) * DH],
                           start=False, stop=(hh == 7)))(),
                       reads=[qdfmb, Sbuf], writes=[psO1b], tag=f"boc_{t}_{c}_{h}")
            p.emit("vector",
                   lambda e: e.tensor_add(S_sb[0:64, 0:512], S_sb[0:64, 0:512], psKV[0:64, :]),
                   reads=[Sbuf, psKVb], writes=[Sbuf], tag=f"bSu2_{t}_{c}")
            for hh in range(8):
                h = 8 + hh
                p.emit("tensor",
                       (lambda c=c, hh=hh, h=h: lambda e: e.matmul(
                           psKV[0:64, hh * DH:(hh + 1) * DH],
                           kd_tm[:, c * D + h * DH:c * D + (h + 1) * DH],
                           v_bf[:, c * D + h * DH:c * D + (h + 1) * DH],
                           start=(hh == 0), stop=(hh == 7)))(),
                       reads=[kdtmb, vbfb], writes=[psKVb], tag=f"bkv_{t}_{c}_{h}")
            p.emit("scalar",
                   lambda e: e.copy(o_sb[:, 512:1024], psO1[:, 0:512]),
                   reads=[psO1b], writes=[osbb], tag=f"boev1_{t}_{c}")
            p.emit("scalar", lambda e: e.activation(gscr2[:], gscr2[:], AF.Exp),
                   reads=[gscr2b], writes=[gscr2b], tag=f"bsg_{t}_{c}")
            p.emit("vector", lambda e: e.tensor_mul(gsil[:], g_sb[:], gscr2[:]),
                   reads=[gb, gscr2b], writes=[gsilb], tag=f"bgsl_{t}_{c}")
            p.emit("vector",
                   lambda e: e.tensor_add(S_sb[0:64, 512:1024], S_sb[0:64, 512:1024], psKV[0:64, :]),
                   reads=[Sbuf, psKVb], writes=[Sbuf], tag=f"bSu2b_{t}_{c}")
            if blevel < 12:
                return
            p.emit("scalar",
                   lambda e: e.activation(oscr[:, 512:1024], o_sb[:, 512:1024], AF.Square),
                   reads=[osbb], writes=[oscrb], tag=f"bosq1_{t}_{c}")
            p.emit("vector",
                   lambda e: e.tensor_reduce(
                       oss_sb[:, 8:16][:, :, None], oscr[:, 512:1024].rearrange("p (h dh) -> p h dh", h=8),
                       AX.X, ALU.add),
                   reads=[oscrb], writes=[ossb], tag=f"boss1_{t}_{c}")
            p.emit("scalar",
                   lambda e: e.activation(rfac_sb[:], oss_sb[:], AF.Ln, bias=float(RMS_EPS), scale=1.0 / DH),
                   reads=[ossb], writes=[rfacb], tag=f"borf1_{t}_{c}")
            p.emit("scalar",
                   lambda e: e.activation(rfac_sb[:], rfac_sb[:], AF.Exp, bias=0.0, scale=-0.5),
                   reads=[rfacb], writes=[rfacb], tag=f"borf2_{t}_{c}")
            p.emit("vector",
                   lambda e: e.tensor_tensor(
                       oscr[:].rearrange("p (h dh) -> p h dh", h=NH),
                       o_sb[:].rearrange("p (h dh) -> p h dh", h=NH),
                       bcast_in1(rfac_sb[:][:, :, None], o_sb[:].rearrange("p (h dh) -> p h dh", h=NH)),
                       ALU.mult),
                   reads=[osbb, rfacb], writes=[oscrb], tag=f"bor_{t}_{c}")
            p.emit("vector",
                   lambda e: e.tensor_mul(og_tm[:], oscr[:], gsil[:]),
                   reads=[oscrb, gsilb], writes=[ogtmb], tag=f"bog_{t}_{c}")
            if debug and t < 2:
                di = t * 2 + c
                p.emit("gpsimd", (lambda di=di: lambda e: e.dma_start(out=dbg_bf[di + 12, :, 0:D], in_=og_tm[:]))(),
                       reads=[ogtmb], dma=True, tag=f"dbgog{t}_{c}")
                p.emit("gpsimd", (lambda di=di: lambda e: e.dma_start(out=dbg_d[di + 4, :, 0:D], in_=oscr[:]))(),
                       reads=[oscrb], dma=True, tag=f"dbgoscr{t}_{c}")
            if nk is not None:
                emit_kproj(nk)
                _emit_vproj(nk, 1)
            for kc in range(8):
                p.emit("tensor",
                       (lambda kc=kc: lambda e: e.transpose(
                           ps_tp[:, kc * 128:(kc + 1) * 128],
                           og_tm[:, kc * 128:(kc + 1) * 128], ident_bf[:]))(),
                       reads=[ogtmb, consts], writes=[tpb2], tag=f"bot_{t}_{c}_{kc}")
            p.emit("scalar",
                   lambda e: e.copy(og_fm[:], ps_tp[:]),
                   reads=[tpb2], writes=[ogfmb], tag=f"bote_{t}_{c}")
            if nk is not None:
                _emit_zgt(nk)
            for kc in range(8):
                for half in range(2):
                    p.emit("tensor",
                           (lambda kc=kc, half=half: lambda e: e.matmul(
                               psQG[:, half * 512:(half + 1) * 512],
                               og_fm[:, kc * 128:(kc + 1) * 128],
                               Wo_sb[:, kc * D + half * 512:kc * D + (half + 1) * 512],
                               start=(kc == 0), stop=(kc == 7)))(),
                           reads=[ogfmb, wob], writes=[psQGb], tag=f"bwo_{t}_{c}_{kc}_{half}")
            p.emit("vector",
                   (lambda c=c, xt=xt, x2t=x2t: lambda e: e.tensor_add(
                       x2t[:, c * D:c * D + 512], xt[:, c * D:c * D + 512], psQG[:, 0:512]))(),
                   reads=[psQGb, xb], writes=[x2b], tag=f"bx2_{t}_{c}")
            p.emit("vector",
                   (lambda c=c, xt=xt, x2t=x2t: lambda e: e.tensor_add(
                       x2t[:, c * D + 512:(c + 1) * D], xt[:, c * D + 512:(c + 1) * D], psQG[:, 512:1024]))(),
                   reads=[psQGb, xb], writes=[x2b], tag=f"bx2b_{t}_{c}")
            p.emit("vector",
                   (lambda x2t=x2t, c=c, tc=tc: lambda e: e.tensor_scalar(
                       sqscr[:], x2t[:, c * D:(c + 1) * D], 1.0, 0.0,
                       ALU.mult, ALU.add, accum_out=s13_sb[:, tc:tc + 1]))(),
                   reads=[x2b], writes=[sqb, stats3], tag=f"b31_{t}_{c}")
            p.emit("vector",
                   (lambda x2t=x2t, c=c, tc=tc: lambda e: e.scalar_tensor_tensor(
                       sqscr[:], x2t[:, c * D:(c + 1) * D], 0.0, x2t[:, c * D:(c + 1) * D],
                       ALU.add, ALU.mult, accum_out=s23_sb[:, tc:tc + 1]))(),
                   reads=[x2b], writes=[sqb, stats3], tag=f"b32_{t}_{c}")
            p.emit("vector",
                   lambda e: e.tensor_tensor(
                       S_sb[:].rearrange("p (g dh) -> p g dh", g=NH),
                       S_sb[:].rearrange("p (g dh) -> p g dh", g=NH),
                       bcast_in1(eBexp_sb[0:64, :][:, :, None], S_sb[:].rearrange("p (g dh) -> p g dh", g=NH)),
                       ALU.mult),
                   reads=[Sbuf, eB2b], writes=[Sbuf], tag=f"bSu1_{t}_{c}")
            p.emit("vector",
                   lambda e: e.tensor_copy(S_bf[:], S_sb[:]),
                   reads=[Sbuf], writes=[Sbuf], tag=f"bSu3_{t}_{c}")
            if nk is not None:
                proj_block(nk, only_q=True)

        for t in range(n_tiles + 1):
            if t < n_tiles:
                bi = t % 2
                xb, xt = x1sbb[bi], x1sb[bi]
                tok0 = t * TILE
                p.emit("sync",
                       (lambda xt=xt, tok0=tok0: lambda e: e.dma_start(
                           out=xt[:].rearrange("p (c f) -> p c f", c=2), in_=x1_d[tok0:tok0 + TILE, :].rearrange("(c p) f -> p c f", c=2)))(),
                       writes=[xb], dma=True, tag=f"bld{t}")
            if t > 0 and blevel >= 3:
                ck0, ck1 = mkck(t - 1, 0), mkck(t - 1, 1)
                chunk_compute(ck0, ck1)
            if t < n_tiles:
                for c in range(2):
                    ckl = mkck(t, c)
                    chunk_stats(t, c, ckl["xt"], ckl["xb"], ckl["xbf_fm"], ckl["xbffmb"],
                                ckl["fac_sb"], ckl["facb"], ckl["stt_sb"], ckl["sttb"],
                                ckl["nmrow"], ckl["nmrowb"])
            if blevel < 3:
                continue
            if t == 0:
                proj_block(mkck(0, 0))
                continue
            ck1 = mkck(t - 1, 1)
            nk2 = mkck(t, 0) if t < n_tiles else None
            chunk_compute(ck1, nk2)
            if blevel >= 14:
                tok0 = (t - 1) * TILE
                p.emit("gpsimd",
                       (lambda x2t=ck1["x2t"], tok0=tok0: lambda e: e.dma_start(
                           out=x2_d[tok0:tok0 + TILE, :].rearrange("(c p) f -> p c f", c=2), in_=x2t[:].rearrange("p (c f) -> p c f", c=2)))(),
                       reads=[ck1["x2b"]], dma=True, tag=f"bst{t - 1}")
        emit_stat_finish(s13_sb, s23_sb, stats3, m3_sb, r3_sb, stats3, n_cs)

    p.barrier()

    if "C" in phases:
        bw1t = bfv(0, 8 * HD)
        bw2t = bfv(8 * HD, 32 * D)
        bw1b, bw2b = Buf(), Buf()
        for kc in range(8):
            p.emit("sync",
                   (lambda kc=kc: lambda e: e.dma_start(
                       out=bw1t[:, kc * HD:(kc + 1) * HD], in_=bw1t_d[:, kc * HD:(kc + 1) * HD]))(),
                   writes=[bw1b], dma=True, tag=f"cbw1ld{kc}")
            p.emit("sync",
                   (lambda kc=kc: lambda e: e.dma_start(
                       out=bw2t[:, kc * 4 * D:(kc + 1) * 4 * D], in_=bw2t_d[:, kc * 4 * D:(kc + 1) * 4 * D]))(),
                   writes=[bw2b], dma=True, tag=f"cbw2ld{kc}")
        emit_ffn_phase(x2_d, out_d, m3_sb, r3_sb, stats3, bw1t, bw1b, bw2t, bw2b, "fc")

    final = Buf()
    last_store = p.eng_ops["gpsimd"][-1]
    last_store.inc = True

    p.finalize()

    with nc.Block() as blk:
        blk.sync(lambda e: p.replay_engine("sync", e))
        blk.tensor(lambda e: p.replay_engine("tensor", e))
        blk.vector(lambda e: p.replay_engine("vector", e))
        blk.scalar(lambda e: p.replay_engine("scalar", e))

        def _gp(e):
            p.replay_engine("gpsimd", e)
            for i in range(NSLOT):
                fc = p.final_counts.get(f"gpsimd_dma{i}", 0)
                if fc > 0:
                    e.wait_ge(p.sems[f"gpsimd_dma{i}"], fc)
        blk.gpsimd(_gp)

    stP.close()
    return nc


def make_consts():
    return {
        "c_identbf": np.eye(128, dtype=np.float32).astype(BFNP),
        "c_identf": np.eye(128, dtype=np.float32),
        "c_tril": np.tril(np.ones((128, 128), dtype=np.float32)).T.copy(),
        "c_cumu": (np.triu(np.ones((128, 128), dtype=np.float32)) / 16.0),
    }


_NC_CACHE = {}


def kernel(L0_attn_x, L0_attn_Wq, L0_attn_Wk, L0_attn_Wv, L0_attn_Wg, L0_attn_Wgt,
           L0_attn_Wo, L0_ffn_w1, L0_ffn_b1, L0_ffn_w2, L0_ffn_b2):
    x = np.ascontiguousarray(np.asarray(L0_attn_x, dtype=np.float32))
    B, T, _ = x.shape
    n_cores = 8
    per = B // n_cores
    n_tok = per * T
    if n_tok not in _NC_CACHE:
        _NC_CACHE[n_tok] = build_program(n_tok)
    nc = _NC_CACHE[n_tok]
    consts = make_consts()
    weights = {
        "wq": np.ascontiguousarray(np.asarray(L0_attn_Wq, dtype=np.float32)),
        "wk": np.ascontiguousarray(np.asarray(L0_attn_Wk, dtype=np.float32)),
        "wv": np.ascontiguousarray(np.asarray(L0_attn_Wv, dtype=np.float32)),
        "wg": np.ascontiguousarray(np.asarray(L0_attn_Wg, dtype=np.float32)),
        "wgt": np.ascontiguousarray(np.asarray(L0_attn_Wgt, dtype=np.float32)),
        "wo": np.ascontiguousarray(np.asarray(L0_attn_Wo, dtype=np.float32)),
        "w1": np.ascontiguousarray(np.asarray(L0_ffn_w1, dtype=np.float32)),
        "b1": np.ascontiguousarray(np.asarray(L0_ffn_b1, dtype=np.float32)),
        "w2": np.ascontiguousarray(np.asarray(L0_ffn_w2, dtype=np.float32)),
        "b2": np.ascontiguousarray(np.asarray(L0_ffn_b2, dtype=np.float32)),
    }
    in_maps = []
    for c in range(n_cores):
        im = {"x": np.ascontiguousarray(x[c * per:(c + 1) * per].reshape(n_tok, D))}
        im.update(weights)
        im.update(consts)
        in_maps.append(im)
    res = run_bass_kernel_spmd(nc, in_maps, core_ids=list(range(n_cores)))
    outs = [res.results[c]["out"].reshape(per, T, D) for c in range(n_cores)]
    return np.concatenate(outs, axis=0)


# version 20 (final)
# speedup vs baseline: 1.0752x; 1.0752x over previous
import numpy as np
import ml_dtypes
import contextlib
import os as _os
import concourse.bass as bass
import concourse.mybir as mybir
from concourse.bass import broadcast_tensor_aps
from concourse.bass_utils import run_bass_kernel_spmd

F32 = mybir.dt.float32
BF16 = mybir.dt.bfloat16
AF = mybir.ActivationFunctionType
ALU = mybir.AluOpType
AX = mybir.AxisListType
BFNP = ml_dtypes.bfloat16

D = 1024
HD = 4096
NH = 16
DH = 64
CH = 128
TILE = 256
LN_EPS = 1e-5
RMS_EPS = 1e-6
LOG_QSCALE = float(np.log(np.float64(DH) ** -0.5))

ENGS = ("sync", "tensor", "vector", "scalar", "gpsimd")


class Op:
    __slots__ = ("eng", "fn", "deps", "dma", "inc", "sem_val", "idx", "tag", "slot")

    def __init__(self, eng, fn, dma, idx, tag=""):
        self.eng = eng
        self.fn = fn
        self.deps = []
        self.dma = dma
        self.inc = False
        self.sem_val = None
        self.idx = idx
        self.tag = tag
        self.slot = None


class Buf:
    __slots__ = ("w", "r")

    def __init__(self):
        self.w = None
        self.r = []


NSLOT = 8


class Prog:
    def __init__(self, nc):
        self.nc = nc
        self.ops = []
        self.eng_ops = {e: [] for e in ENGS}
        self.sems = {
            "tensor": nc.alloc_semaphore("sPE"),
            "vector": nc.alloc_semaphore("sVE"),
            "scalar": nc.alloc_semaphore("sSE"),
            "gpsimd": nc.alloc_semaphore("sGP"),
        }
        for q in ("sync", "gpsimd"):
            for i in range(NSLOT):
                self.sems[f"{q}_dma{i}"] = nc.alloc_semaphore(f"s_{q}_d{i}")
        self._dma_seq = {"sync": 0, "gpsimd": 0}
        self._last_per_stream = {}
        self._barrier = None
        self._barrier_seen = set()

    def _stream(self, op):
        if op.dma:
            return f"{op.eng}_dma{op.slot}"
        return op.eng

    def emit(self, eng, fn, reads=(), writes=(), deps=(), dma=False, tag=""):
        o = Op(eng, fn, dma, len(self.ops), tag)
        if dma:
            o.slot = self._dma_seq[eng] % NSLOT
            self._dma_seq[eng] += 1
        self._last_per_stream[self._stream(o)] = o
        dl = list(deps)
        for b in reads:
            if b.w is not None:
                dl.append(b.w)
        for b in writes:
            dl.extend(b.r)
            if b.w is not None:
                dl.append(b.w)
        if self._barrier is not None and eng not in self._barrier_seen:
            dl.extend(self._barrier)
            self._barrier_seen.add(eng)
        o.deps = [d for d in dl if d is not None and d is not o]
        self.ops.append(o)
        self.eng_ops[eng].append(o)
        for b in reads:
            b.r.append(o)
        for b in writes:
            b.w = o
            b.r = []
        return o

    def barrier(self):
        self._barrier = list(self._last_per_stream.values())
        self._barrier_seen = set()

    def finalize(self):
        for o in self.ops:
            best = {}
            for d in o.deps:
                if d.eng == o.eng and not d.dma and d.eng in ("tensor", "sync"):
                    continue
                s = self._stream(d)
                if s not in best or d.idx > best[s].idx:
                    best[s] = d
            o.deps = list(best.values())
            for d in o.deps:
                d.inc = True
        counts = {s: 0 for s in self.sems}
        for e in ENGS:
            for o in self.eng_ops[e]:
                if o.dma:
                    o.inc = True
                if o.inc:
                    s = self._stream(o)
                    counts[s] += 16 if o.dma else 1
                    o.sem_val = counts[s]
        self.final_counts = counts

    def replay_engine(self, eng_name, eng):
        waited = {}
        for o in self.eng_ops[eng_name]:
            for d in sorted(o.deps, key=lambda x: x.idx):
                s = self._stream(d)
                v = d.sem_val
                if waited.get(s, 0) < v:
                    eng.wait_ge(self.sems[s], v)
                    waited[s] = v
            if o.dma:
                s = self._stream(o)
                v = o.sem_val - 16
                if v > 0 and waited.get(s, 0) < v:
                    eng.wait_ge(self.sems[s], v)
                    waited[s] = v
            inst = o.fn(eng)
            if o.inc:
                inst.then_inc(self.sems[self._stream(o)], 16 if o.dma else 1)


def bcast_in1(in1_ap, like_ap):
    b, _ = broadcast_tensor_aps(in1_ap, like_ap)
    return b


def build_program(n_tok, audit=False, debug=False, phases="0ABC", blevel=99):
    assert n_tok % TILE == 0
    seq_len = min(n_tok, 4096)
    n_tiles = n_tok // TILE
    tiles_per_seq = seq_len // TILE
    n_cs = n_tok // CH

    nc = bass.Bass("TRN2", target_bir_lowering=False, detect_race_conditions=audit)
    p = Prog(nc)
    p.audit = audit

    x_d = nc.dram_tensor("x", [n_tok, D], F32, kind="ExternalInput").ap()
    wq_d = nc.dram_tensor("wq", [D, D], F32, kind="ExternalInput").ap()
    wk_d = nc.dram_tensor("wk", [D, D], F32, kind="ExternalInput").ap()
    wv_d = nc.dram_tensor("wv", [D, D], F32, kind="ExternalInput").ap()
    wg_d = nc.dram_tensor("wg", [D, D], F32, kind="ExternalInput").ap()
    wgt_d = nc.dram_tensor("wgt", [D, NH], F32, kind="ExternalInput").ap()
    wo_d = nc.dram_tensor("wo", [D, D], F32, kind="ExternalInput").ap()
    w1_d = nc.dram_tensor("w1", [HD, D], F32, kind="ExternalInput").ap()
    b1_d = nc.dram_tensor("b1", [HD], F32, kind="ExternalInput").ap()
    w2_d = nc.dram_tensor("w2", [D, HD], F32, kind="ExternalInput").ap()
    b2_d = nc.dram_tensor("b2", [D], F32, kind="ExternalInput").ap()
    identbf_d = nc.dram_tensor("c_identbf", [128, 128], BF16, kind="ExternalInput").ap()
    identf_d = nc.dram_tensor("c_identf", [128, 128], F32, kind="ExternalInput").ap()
    tril_d = nc.dram_tensor("c_tril", [128, 128], F32, kind="ExternalInput").ap()
    cumu_d = nc.dram_tensor("c_cumu", [128, 128], F32, kind="ExternalInput").ap()
    out_d = nc.dram_tensor("out", [n_tok, D], F32, kind="ExternalOutput").ap()
    _scr_kind = "ExternalOutput" if _os.environ.get("KSCROUT") else "Internal"
    x1_d = nc.dram_tensor("x1_scr", [n_tok, D], F32, kind=_scr_kind).ap()
    x2_d = nc.dram_tensor("x2_scr", [n_tok, D], F32, kind=_scr_kind).ap()
    bw1t_d = nc.dram_tensor("bw1t_scr", [128, 8 * HD], BF16, kind=_scr_kind).ap()
    if _os.environ.get("KSCROUT"):
        st_d = nc.dram_tensor("stats_out", [128, 256], F32, kind="ExternalOutput").ap()
        ff_d = nc.dram_tensor("ff_out", [8, 128, 2 * D], BF16, kind="ExternalOutput").ap()
    if debug:
        dbg_d = nc.dram_tensor("dbg", [64, 128, 2 * D], F32, kind="ExternalOutput").ap()
        dbg_bf = nc.dram_tensor("dbgb", [64, 128, 2 * D], BF16, kind="ExternalOutput").ap()
    bw2t_d = nc.dram_tensor("bw2t_scr", [128, 32 * D], BF16, kind=_scr_kind).ap()

    stP = contextlib.ExitStack()

    def sbP(name, shape, dt):
        return stP.enter_context(nc.sbuf_tensor(name, shape, dt))

    ident_bf = sbP("ident_bf", [128, 128], BF16)
    ident_f = sbP("ident_f", [128, 128], F32)
    tril_f = sbP("tril_f", [128, 128], F32)
    cumu_f = sbP("cumu_f", [128, 128], F32)
    onesP_bf = sbP("onesP_bf", [128, 1], BF16)
    onesP16_f = sbP("onesP16_f", [128, 1], F32)
    ones1_bf = sbP("ones1_bf", [1, 128], BF16)
    ones1_f = sbP("ones1_f", [1, 128], F32)
    onesrow_bf = sbP("onesrow_bf", [1, TILE], BF16)
    m1_sb = sbP("m1", [128, n_cs], F32)
    r1_sb = sbP("r1", [128, n_cs], F32)
    m3_sb = sbP("m3", [128, n_cs], F32)
    r3_sb = sbP("r3", [128, n_cs], F32)
    s13_sb = sbP("s13", [128, n_cs], F32)
    s23_sb = sbP("s23", [128, n_cs], F32)
    S_sb = sbP("S_state", [64, NH * DH], F32)
    S_bf = sbP("S_state_bf", [64, NH * DH], BF16)
    cvals = sbP("cvals", [128, 4], F32)
    b1_cols = sbP("b1_cols", [128, 32], F32)
    pool_bf = sbP("pool_bf", [128, 72384], BF16)
    pool_f = sbP("pool_f", [128, 13568], F32)

    def bfv(base, n, rows=None):
        if rows is None:
            return pool_bf[:, base:base + n]
        return pool_bf[0:rows, base:base + n]

    def f32v(base, n, rows=None):
        if rows is None:
            return pool_f[:, base:base + n]
        return pool_f[0:rows, base:base + n]
    b2_bf = sbP("b2_bf", [1, D], BF16)
    fin_sb = sbP("fin", [128, n_cs], F32)
    eBbc_sb = sbP("eBbc", [128, NH], F32)
    pbig0 = stP.enter_context(nc.psum_tensor("pbig0", [128, 1024], F32))
    pbig1 = stP.enter_context(nc.psum_tensor("pbig1", [128, 1024], F32))
    phalf0 = stP.enter_context(nc.psum_tensor("phalf0", [128, 512], F32))
    phalf1 = stP.enter_context(nc.psum_tensor("phalf1", [128, 512], F32))
    ptp = stP.enter_context(nc.psum_tensor("ptp", [128, 1024], BF16))
    ptiny = stP.enter_context(nc.psum_tensor("ptiny", [128, 512], F32))

    consts = Buf()
    tinyb0 = Buf()
    stats1 = Buf()
    stats3 = Buf()
    Sbuf = Buf()
    bias_bf = Buf()
    finb = Buf()

    p.emit("sync", lambda e: e.dma_start(out=ident_bf[:], in_=identbf_d), writes=[consts], dma=True)
    p.emit("sync", lambda e: e.dma_start(out=ident_f[:], in_=identf_d), writes=[consts], dma=True)
    p.emit("sync", lambda e: e.dma_start(out=tril_f[:], in_=tril_d), writes=[consts], dma=True)
    p.emit("sync", lambda e: e.dma_start(out=cumu_f[:], in_=cumu_d), writes=[consts], dma=True)
    p.emit("gpsimd", lambda e: e.dma_start(out=b2_bf[:], in_=b2_d[None, :]), writes=[bias_bf], dma=True)
    p.emit("vector", lambda e: e.memset(onesP_bf[:], 1.0), writes=[consts])
    p.emit("vector", lambda e: e.memset(onesP16_f[:], 1.0 / 16.0), writes=[consts])
    p.emit("vector", lambda e: e.memset(ones1_bf[:], 1.0), writes=[consts])
    p.emit("vector", lambda e: e.memset(ones1_f[:], 1.0), writes=[consts])
    p.emit("vector", lambda e: e.memset(onesrow_bf[:], 1.0), writes=[consts])
    p.emit("vector", lambda e: e.memset(S_sb[:], 0.0), writes=[Sbuf])
    p.emit("vector", lambda e: e.memset(S_bf[:], 0.0), writes=[Sbuf])
    _cv = [float(LN_EPS), 1.0, float(LOG_QSCALE), float(RMS_EPS)]
    for _i, _v in enumerate(_cv):
        p.emit("vector", (lambda _i=_i, _v=_v: lambda e: e.memset(cvals[:, _i:_i + 1], _v))(), writes=[consts])
        nc.const_aps.register(F32, _v, cvals[:, _i:_i + 1])

    def emit_stat_finish(s1g, s2g, gsrc, mg, rg, gdst, n):
        p.emit("vector", lambda e: e.tensor_scalar(mg[:, 0:n], s1g[:, 0:n], 1.0 / D, None, ALU.mult),
               reads=[gsrc], writes=[gdst])
        p.emit("vector", lambda e: e.tensor_tensor(fin_sb[:, 0:n], mg[:, 0:n], mg[:, 0:n], ALU.mult),
               reads=[gdst], writes=[finb])
        p.emit("vector", lambda e: e.scalar_tensor_tensor(fin_sb[:, 0:n], s2g[:, 0:n], 1.0 / D, fin_sb[:, 0:n], ALU.mult, ALU.subtract),
               reads=[gsrc, finb], writes=[finb])
        p.emit("scalar", lambda e: e.activation(fin_sb[:, 0:n], fin_sb[:, 0:n], AF.Ln, bias=float(LN_EPS), scale=1.0),
               reads=[finb], writes=[finb])
        p.emit("scalar", lambda e: e.activation(rg[:, 0:n], fin_sb[:, 0:n], AF.Exp, bias=0.0, scale=-0.5),
               reads=[finb], writes=[gdst])

    def emit_ffn_phase(src_d, dst_d, mg, rg, statbuf, bw1t, bw1b, bw2t, bw2b, tagp):
        xsb = [f32v(0, 2048), f32v(2048, 2048)]
        xsbb = [Buf() for _ in range(2)]
        xln = bfv(65536, 2048)
        xlnb = Buf()
        xfmB = [bfv(67584, 2048), bfv(69632, 2048)]
        xfmbB = [Buf(), Buf()]
        hbf = [bfv(71680, 256), bfv(71936, 256)]
        hbfb = [Buf() for _ in range(2)]
        x1o = [f32v(4096, 2048), f32v(6144, 2048)]
        x1ob = [Buf() for _ in range(2)]
        hp = [phalf0, phalf1]
        hpb = [Buf() for _ in range(2)]
        op_ps = [pbig0, pbig1]
        opb = [Buf() for _ in range(2)]
        tp2 = ptp
        tp2b = Buf()

        for t in range(n_tiles):
            bi = t % 2
            xb, xt = xsbb[bi], xsb[bi]
            ob, ot = x1ob[bi], x1o[bi]
            xfm, xfmb = xfmB[bi], xfmbB[bi]
            tok0 = t * TILE
            p.emit("sync",
                   (lambda xt=xt, tok0=tok0: lambda e: e.dma_start(
                       out=xt[:].rearrange("p (c f) -> p c f", c=2), in_=src_d[tok0:tok0 + TILE, :].rearrange("(c p) f -> p c f", c=2)))(),
                   writes=[xb], dma=True, tag=f"{tagp}ld{t}")
            for c in range(2):
                tc = t * 2 + c
                p.emit("vector",
                       (lambda xt=xt, c=c, tc=tc: lambda e: e.tensor_scalar(
                           xln[:, c * D:(c + 1) * D], xt[:, c * D:(c + 1) * D],
                           mg[:, tc:tc + 1], rg[:, tc:tc + 1], ALU.subtract, ALU.mult))(),
                       reads=[xb, statbuf], writes=[xlnb], tag=f"{tagp}ln{t}_{c}")
            for c in range(2):
                for kc in range(8):
                    p.emit("tensor",
                           (lambda c=c, kc=kc: lambda e: e.transpose(
                               tp2[:, kc * 128:(kc + 1) * 128],
                               xln[:, c * D + kc * 128:c * D + (kc + 1) * 128], ident_bf[:]))(),
                           reads=[xlnb, consts], writes=[tp2b], tag=f"{tagp}tp{t}_{c}_{kc}")
                p.emit("scalar",
                       (lambda c=c: lambda e: e.copy(
                           xfm[:].rearrange("p (kc tt) -> p kc tt", kc=8)[:, :, c * 128:(c + 1) * 128],
                           tp2[:].rearrange("p (kc m) -> p kc m", kc=8)))(),
                       reads=[tp2b], writes=[xfmb], tag=f"{tagp}tpev{t}_{c}")
            if t == 0 and _os.environ.get("KDUMPFF"):
                p.emit("gpsimd", lambda e: e.dma_start(out=ff_d[0], in_=xln[:]), reads=[xlnb], dma=True, tag="dump_xln")
                p.emit("gpsimd", lambda e: e.dma_start(out=ff_d[1], in_=xfm[:]), reads=[xfmb], dma=True, tag="dump_xfm")
            for hcx in range(33):
                if hcx < 32:
                    hc = hcx
                    hpi = hc % 2
                    for kc in range(8):
                        p.emit("tensor",
                               (lambda hpi=hpi, kc=kc, hc=hc: lambda e: e.matmul(
                                   hp[hpi][:, 0:TILE],
                                   bw1t[:, kc * HD + hc * 128:kc * HD + (hc + 1) * 128],
                                   xfm[:, kc * TILE:(kc + 1) * TILE],
                                   start=(kc == 0), stop=(kc == 7)))(),
                               reads=[bw1b, xfmb], writes=[hpb[hpi]], tag=f"{tagp}hmm{t}_{hc}_{kc}")
                    p.emit("scalar",
                           (lambda hpi=hpi, hc=hc: lambda e: e.activation(
                               hbf[hpi][:], hp[hpi][:, 0:TILE], AF.Gelu, bias=b1_cols[:, hc:hc + 1], scale=1.0))(),
                           reads=[hpb[hpi], bias_bf], writes=[hbfb[hpi]], tag=f"{tagp}gelu{t}_{hc}")
                if hcx >= 1:
                    hc = hcx - 1
                    hpi = hc % 2
                    for c in range(2):
                        for nh in range(2):
                            p.emit("tensor",
                                   (lambda c=c, nh=nh, hc=hc, hpi=hpi: lambda e: e.matmul(
                                       op_ps[c][:, nh * 512:(nh + 1) * 512],
                                       hbf[hpi][:, c * 128:(c + 1) * 128],
                                       bw2t[:, hc * D + nh * 512:hc * D + (nh + 1) * 512],
                                       start=(hc == 0), stop=False))(),
                                   reads=[hbfb[hpi], bw2b], writes=[opb[c]], tag=f"{tagp}omm{t}_{hc}_{c}_{nh}")
            for c in range(2):
                for nh in range(2):
                    p.emit("tensor",
                           (lambda c=c, nh=nh: lambda e: e.matmul(
                               op_ps[c][:, nh * 512:(nh + 1) * 512],
                               ones1_bf[:], b2_bf[:, nh * 512:(nh + 1) * 512],
                               start=False, stop=True))(),
                           reads=[bias_bf, consts], writes=[opb[c]], tag=f"{tagp}ob{t}_{c}_{nh}")
                p.emit("vector",
                       (lambda c=c, ot=ot, xt=xt: lambda e: e.scalar_tensor_tensor(
                           ot[:, c * D:(c + 1) * D], op_ps[c][:], 0.5, xt[:, c * D:(c + 1) * D],
                           ALU.mult, ALU.add))(),
                       reads=[opb[c], xb], writes=[ob], tag=f"{tagp}res{t}_{c}")
            p.emit("gpsimd",
                   (lambda ot=ot, tok0=tok0: lambda e: e.dma_start(
                       out=dst_d[tok0:tok0 + TILE, :].rearrange("(c p) f -> p c f", c=2), in_=ot[:].rearrange("p (c f) -> p c f", c=2)))(),
                   reads=[ob], dma=True, tag=f"{tagp}st{t}")
            if _os.environ.get("KBAR"):
                p.barrier()

    if True:
      bw1t = bfv(0, 8 * HD)
      bw2t = bfv(8 * HD, 32 * D)
      bw1b, bw2b = Buf(), Buf()
      if True:
        xst = [f32v(0, D), f32v(D, D), f32v(2 * D, D)]
        xstb = [Buf() for _ in range(3)]
        scr_bf = bfv(65536, D)
        scrb = Buf()
        scr2_bf = bfv(62608, D)
        scr2b = Buf()
        s1g = f32v(5120, n_cs)
        s2g = f32v(5184, n_cs)
        rawb = Buf()
        def p0_stats(tc):
            bi = tc % 3
            xb, xt = xstb[bi], xst[bi]
            p.emit("sync",
                   (lambda xt=xt, tc=tc: lambda e: e.dma_start(out=xt[:], in_=x_d[tc * 128:(tc + 1) * 128, :]))(),
                   writes=[xb], dma=True, tag=f"p0ld{tc}")
            p.emit("vector",
                   (lambda xt=xt, tc=tc: lambda e: e.tensor_scalar(
                       scr_bf[:], xt[:], 1.0, 0.0, ALU.mult, ALU.add, accum_out=s1g[:, tc:tc + 1]))(),
                   reads=[xb], writes=[scrb, rawb], tag=f"p0s1_{tc}")
            p.emit("vector",
                   (lambda xt=xt, tc=tc: lambda e: e.scalar_tensor_tensor(
                       scr2_bf[:], xt[:], 0.0, xt[:], ALU.add, ALU.mult, accum_out=s2g[:, tc:tc + 1]))(),
                   reads=[xb], writes=[scr2b, rawb], tag=f"p0s2_{tc}")
        if _os.environ.get("KDUMPST"):
            p.emit("gpsimd", lambda e: e.dma_start(out=st_d[:, 0:n_cs], in_=m1_sb[:, 0:n_cs]), reads=[stats1], dma=True, tag="dump_m1")
            p.emit("gpsimd", lambda e: e.dma_start(out=st_d[:, 64:64 + n_cs], in_=r1_sb[:, 0:n_cs]), reads=[stats1], dma=True, tag="dump_r1")
            p.emit("gpsimd", lambda e: e.dma_start(out=st_d[:, 128:128 + n_cs], in_=s1g[:, 0:n_cs]), reads=[rawb], dma=True, tag="dump_s1")
            p.emit("gpsimd", lambda e: e.dma_start(out=st_d[:, 192:192 + n_cs], in_=s2g[:, 0:n_cs]), reads=[rawb], dma=True, tag="dump_s2")

        b1row = f32v(8192, HD, rows=1)
        b1rowb = Buf()
        p.emit("sync", lambda e: e.dma_start(out=b1row[:], in_=b1_d[None, :]), writes=[b1rowb], dma=True, tag="b1ld")
        for hc in range(32):
            p.emit("tensor",
                   (lambda hc=hc: lambda e: e.transpose(
                       ptiny[:, hc:hc + 1], b1row[:, hc * 128:(hc + 1) * 128], ones1_f[0:1, 0:1]))(),
                   reads=[b1rowb, consts], writes=[tinyb0], tag=f"b1tp{hc}")
        p.emit("vector", lambda e: e.tensor_copy(b1_cols[:], ptiny[:, 0:32]),
               reads=[tinyb0], writes=[bias_bf], tag="b1ev")
        wst = [f32v(3072, D), f32v(4096, D)]
        wstb = [Buf() for _ in range(2)]
        sgn = bfv(66560, D)
        sgnb = Buf()
        bwc = bfv(67584, D)
        bwcb = Buf()
        acol = f32v(5264, 1)
        acolb = Buf()
        acc4 = f32v(5280, 4)
        acc4b = Buf()
        tp_ps = ptp
        tpb = Buf()

        def p0_w1(rc):
            bi = rc % 2
            wt, wb = wst[bi], wstb[bi]
            p.emit("sync",
                   (lambda wt=wt, rc=rc: lambda e: e.dma_start(out=wt[:], in_=w1_d[rc * 128:(rc + 1) * 128, :]))(),
                   writes=[wb], dma=True, tag=f"w1ld{rc}")
            p.emit("vector",
                   (lambda wt=wt: lambda e: e.reduce_sum(acol[:], wt[:], AX.X, apply_absolute_value=True))(),
                   reads=[wb], writes=[acolb], tag=f"w1abs{rc}")
            p.emit("scalar",
                   (lambda wt=wt: lambda e: e.activation(sgn[:], wt[:], AF.Sign))(),
                   reads=[wb], writes=[sgnb], tag=f"w1sgn{rc}")
            p.emit("vector",
                   lambda e: e.tensor_scalar(bwc[:], sgn[:], acol[:], 1.0 / D, ALU.mult, ALU.mult),
                   reads=[sgnb, acolb], writes=[bwcb], tag=f"w1bw{rc}")
            for kc in range(8):
                p.emit("tensor",
                       (lambda kc=kc: lambda e: e.transpose(
                           tp_ps[:, kc * 128:(kc + 1) * 128], bwc[:, kc * 128:(kc + 1) * 128], ident_bf[:]))(),
                       reads=[bwcb, consts], writes=[tpb], tag=f"w1tp{rc}_{kc}")
            p.emit("scalar",
                   (lambda rc=rc: lambda e: e.copy(
                       bw1t[:].rearrange("p (kc m) -> p kc m", kc=8)[:, :, rc * 128:(rc + 1) * 128],
                       tp_ps[:].rearrange("p (kc m) -> p kc m", kc=8)))(),
                   reads=[tpb], writes=[bw1b], tag=f"w1ev{rc}")

        for tc in range(n_cs):
            p0_stats(tc)
            if tc % 2 == 1 and tc // 2 < 32:
                p0_w1(tc // 2)
        for rc in range(min(n_cs // 2, 32), 32):
            p0_w1(rc)
        emit_stat_finish(s1g, s2g, rawb, m1_sb, r1_sb, stats1, n_cs)
        w2q = [f32v(0, D), f32v(D, D), f32v(2 * D, D), f32v(3072, D)]
        w2qb = [xstb[0], xstb[1], xstb[2], wstb[0]]
        for rc in range(8):
            for qtr in range(4):
                wt, wb = w2q[qtr], w2qb[qtr]
                p.emit("sync",
                       (lambda wt=wt, rc=rc, q=qtr: lambda e: e.dma_start(
                           out=wt[:], in_=w2_d[rc * 128:(rc + 1) * 128, q * 1024:(q + 1) * 1024]))(),
                       writes=[wb], dma=True, tag=f"w2ld{rc}_{qtr}")
                p.emit("vector",
                       (lambda wt=wt, q=qtr: lambda e: e.reduce_sum(acc4[:, q:q + 1], wt[:], AX.X, apply_absolute_value=True))(),
                       reads=[wb], writes=[acc4b], tag=f"w2abs{rc}_{qtr}")
            p.emit("vector",
                   lambda e: e.reduce_sum(acol[:], acc4[:], AX.X),
                   reads=[acc4b], writes=[acolb], tag=f"w2al{rc}")
            for qtr in range(4):
                wt, wb = w2q[qtr], w2qb[qtr]
                p.emit("scalar",
                       (lambda wt=wt: lambda e: e.activation(sgn[:], wt[:], AF.Sign))(),
                       reads=[wb], writes=[sgnb], tag=f"w2sgn{rc}_{qtr}")
                p.emit("vector",
                       lambda e: e.tensor_scalar(bwc[:], sgn[:], acol[:], 1.0 / HD, ALU.mult, ALU.mult),
                       reads=[sgnb, acolb], writes=[bwcb], tag=f"w2bw{rc}_{qtr}")
                for hc8 in range(8):
                    p.emit("tensor",
                           (lambda hc8=hc8: lambda e: e.transpose(
                               tp_ps[:, hc8 * 128:(hc8 + 1) * 128], bwc[:, hc8 * 128:(hc8 + 1) * 128], ident_bf[:]))(),
                           reads=[bwcb, consts], writes=[tpb], tag=f"w2tp{rc}_{qtr}_{hc8}")
                p.emit("scalar",
                       (lambda rc=rc, q=qtr: lambda e: e.copy(
                           bw2t[:].rearrange("p (hc m) -> p hc m", hc=32)[:, q * 8:(q + 1) * 8, rc * 128:(rc + 1) * 128],
                           tp_ps[:].rearrange("p (hc m) -> p hc m", hc=8)))(),
                       reads=[tpb], writes=[bw2b], tag=f"w2ev{rc}_{qtr}")
        for kc in range(8):
            p.emit("gpsimd",
                   (lambda kc=kc: lambda e: e.dma_start(
                       out=bw1t_d[:, kc * HD:(kc + 1) * HD], in_=bw1t[:, kc * HD:(kc + 1) * HD]))(),
                   reads=[bw1b], dma=True, tag=f"bw1st{kc}")
            p.emit("gpsimd",
                   (lambda kc=kc: lambda e: e.dma_start(
                       out=bw2t_d[:, kc * 4 * D:(kc + 1) * 4 * D], in_=bw2t[:, kc * 4 * D:(kc + 1) * 4 * D]))(),
                   reads=[bw2b], dma=True, tag=f"bw2st{kc}")

      p.barrier()
      if "A" in phases:
          emit_ffn_phase(x_d, x1_d, m1_sb, r1_sb, stats1, bw1t, bw1b, bw2t, bw2b, "fa")
    p.barrier()

    if "B" in phases:
        Wq_sb = bfv(0, 8 * D)
        Wk_sb = bfv(8 * D, 8 * D)
        Wv_sb = bfv(16 * D, 8 * D)
        Wg_sb = bfv(24 * D, 8 * D)
        Wo_sb = bfv(32 * D, 8 * D)
        Wgt_sb = bfv(40 * D, 8 * NH)
        cw_sb = bfv(41088, 4 * D + NH, rows=1)
        wqb, wkb, wvb, wgb, wob, wgtb, cwb = (Buf() for _ in range(7))
        wmap = [(wq_d, Wq_sb, wqb), (wk_d, Wk_sb, wkb), (wv_d, Wv_sb, wvb),
                (wg_d, Wg_sb, wgb), (wo_d, Wo_sb, wob)]
        for wd, wsb_, wb_ in wmap:
            for kc in range(8):
                p.emit("gpsimd",
                       (lambda wd=wd, wsb_=wsb_, kc=kc: lambda e: e.dma_start(
                           out=wsb_[:, kc * D:(kc + 1) * D], in_=wd[kc * 128:(kc + 1) * 128, :]))(),
                       writes=[wb_], dma=True, tag="wld")
        for kc in range(8):
            p.emit("gpsimd",
                   (lambda kc=kc: lambda e: e.dma_start(
                       out=Wgt_sb[:, kc * NH:(kc + 1) * NH], in_=wgt_d[kc * 128:(kc + 1) * 128, :]))(),
                   writes=[wgtb], dma=True, tag="wgtld")

        ps_tiny = ptiny
        ps_tp = ptp
        psQG = pbig0
        psKAT = pbig1
        psO1 = phalf0
        psKV = phalf1
        tinyb, tpb2, psQGb, psO1b, psKVb = (Buf() for _ in range(5))
        psKATbH = [Buf(), Buf()]

        for wi, (_, wsb_, wb_) in enumerate(wmap[:4]):
            for half in range(2):
                for kc in range(8):
                    p.emit("tensor",
                           (lambda wsb_=wsb_, kc=kc, half=half: lambda e: e.matmul(
                               ps_tiny[0:1, 0:512], onesP_bf[:],
                               wsb_[:, kc * D + half * 512:kc * D + (half + 1) * 512],
                               start=(kc == 0), stop=(kc == 7)))(),
                           reads=[wb_, consts], writes=[tinyb], tag="cwmm")
                p.emit("vector",
                       (lambda wi=wi, half=half: lambda e: e.tensor_copy(
                           cw_sb[:, wi * D + half * 512:wi * D + (half + 1) * 512], ps_tiny[0:1, 0:512]))(),
                       reads=[tinyb], writes=[cwb], tag="cwev")
        for kc in range(8):
            p.emit("tensor",
                   (lambda kc=kc: lambda e: e.matmul(
                       ps_tiny[0:1, 0:NH], onesP_bf[:], Wgt_sb[:, kc * NH:(kc + 1) * NH],
                       start=(kc == 0), stop=(kc == 7)))(),
                   reads=[wgtb, consts], writes=[tinyb], tag="cwgtmm")
        p.emit("vector",
               lambda e: e.tensor_copy(cw_sb[:, 4 * D:4 * D + NH], ps_tiny[0:1, 0:NH]),
               reads=[tinyb], writes=[cwb], tag="cwgtev")

        x1sb = [f32v(0, 2048), f32v(2048, 2048)]
        x1sbb = [Buf() for _ in range(2)]
        xbf_tm = bfv(45200, 2048)
        xbftmb = Buf()
        xbf_fmB = [bfv(47248, 2048), bfv(69008, 2048)]
        xbffmbB = [Buf(), Buf()]
        sqscr = bfv(49296, D)
        sqb = Buf()
        sqscr2 = bfv(71312, D)
        sq2b = Buf()
        o_sb = f32v(12544, D)
        osbb = Buf()
        qd_tm = bfv(50320, 2048)
        qdtmb = Buf()
        kd_tm = bfv(52368, 2048)
        kdtmb = Buf()
        qd_fm = bfv(54416, 4096, rows=64)
        qdfmb = Buf()
        kd_fm = bfv(58512, 4096, rows=64)
        kdfmb = Buf()
        v_bf = bfv(62608, 2048)
        vbfb = Buf()
        g_sb = f32v(8192, D)
        gb = Buf()
        gscr = f32v(9216, D)
        gscrb = Buf()
        gscr2 = f32v(10240, D)
        gscr2b = Buf()
        gsil = bfv(64656, D)
        gsilb = Buf()
        og_tm = bfv(65680, D)
        ogtmb = Buf()
        og_fm = bfv(66704, D)
        ogfmb = Buf()
        A_sb = bfv(67728, 1024)
        Ab = [Buf() for _ in range(8)]
        oscr = f32v(11264, D)
        oscrb = Buf()
        x2o = [f32v(4096, 2048), f32v(6144, 2048)]
        x2ob = [Buf() for _ in range(2)]
        nmrowP = [bfv(68752, TILE, rows=1), bfv(71056, TILE, rows=1)]
        nmrowbP = [Buf(), Buf()]
        fac_sbP = [[f32v(12304, 8), f32v(12512, 8)], [f32v(12352, 8), f32v(12400, 8)]]
        facbP = [[Buf(), Buf()], [Buf(), Buf()]]
        stt_sbP = [[f32v(12288, 4), f32v(12520, 4)], [f32v(12360, 4), f32v(12408, 4)]]
        sttbP = [[Buf(), Buf()], [Buf(), Buf()]]
        gt_sb = f32v(12320, NH)
        gtb = Buf()
        gt2_sb = f32v(12336, NH)
        gt2b = Buf()
        E1_sb = f32v(12368, NH)
        E1b = Buf()
        E2_sb = f32v(12384, NH)
        E2b = Buf()
        eBexp_sb = f32v(12416, NH)
        eB2b = Buf()
        Braw_sb = f32v(12496, NH, rows=1)
        Brawb = Buf()
        oss_sb = f32v(12432, NH)
        ossb = Buf()
        rfac_sb = f32v(12448, NH)
        rfacb = Buf()

        def chunk_stats(t, c, xt, xb, xbf_fm, xbffmb, fac_sb, facb, stt_sb, sttb, nmrow, nmrowb):
            p.emit("vector",
                   (lambda xt=xt, c=c: lambda e: e.tensor_scalar(
                       xbf_tm[:, c * D:(c + 1) * D], xt[:, c * D:(c + 1) * D], 1.0, 0.0,
                       ALU.mult, ALU.add, accum_out=fac_sb[:, 7:8]))(),
                   reads=[xb], writes=[xbftmb, facb], tag=f"bc1_{t}_{c}")
            p.emit("vector",
                   (lambda xt=xt, c=c: lambda e: e.scalar_tensor_tensor(
                       sqscr2[:], xt[:, c * D:(c + 1) * D], 0.0, xt[:, c * D:(c + 1) * D],
                       ALU.add, ALU.mult, accum_out=fac_sb[:, 6:7]))(),
                   reads=[xb], writes=[sq2b, facb], tag=f"bc2_{t}_{c}")
            p.emit("vector",
                   lambda e: e.tensor_scalar(fac_sb[:, 5:6], fac_sb[:, 7:8], 1.0 / D, None, ALU.mult),
                   reads=[facb], writes=[facb], tag=f"bm_{t}_{c}")
            p.emit("vector",
                   lambda e: e.tensor_tensor(stt_sb[:, 0:1], fac_sb[:, 5:6], fac_sb[:, 5:6], ALU.mult),
                   reads=[facb], writes=[sttb], tag=f"bm2_{t}_{c}")
            p.emit("vector",
                   lambda e: e.scalar_tensor_tensor(fac_sb[:, 4:5], fac_sb[:, 6:7], 1.0 / D, stt_sb[:, 0:1], ALU.mult, ALU.subtract),
                   reads=[facb, sttb], writes=[facb], tag=f"bvar_{t}_{c}")
            p.emit("scalar",
                   lambda e: e.activation(stt_sb[:, 1:2], fac_sb[:, 4:5], AF.Ln, bias=float(LN_EPS), scale=1.0),
                   reads=[facb], writes=[sttb], tag=f"blnv_{t}_{c}")
            p.emit("scalar",
                   lambda e: e.activation(fac_sb[:, 3:4], stt_sb[:, 1:2], AF.Exp, bias=0.0, scale=-0.5),
                   reads=[sttb], writes=[facb], tag=f"binv_{t}_{c}")
            p.emit("vector",
                   lambda e: e.tensor_scalar(fac_sb[:, 2:3], fac_sb[:, 4:5], float(RMS_EPS), float(RMS_EPS * LN_EPS), ALU.mult, ALU.add),
                   reads=[facb], writes=[facb], tag=f"bs2e_{t}_{c}")
            if blevel < 2:
                return
            for kc in range(8):
                p.emit("tensor",
                       (lambda c=c, kc=kc: lambda e: e.transpose(
                           ps_tp[:, kc * 128:(kc + 1) * 128],
                           xbf_tm[:, c * D + kc * 128:c * D + (kc + 1) * 128], ident_bf[:]))(),
                       reads=[xbftmb, consts], writes=[tpb2], tag=f"bxtp_{t}_{c}_{kc}")
            p.emit("scalar",
                   (lambda c=c: lambda e: e.copy(
                       xbf_fm[:].rearrange("p (kc tt) -> p kc tt", kc=8)[:, :, c * 128:(c + 1) * 128],
                       ps_tp[:].rearrange("p (kc m) -> p kc m", kc=8)))(),
                   reads=[tpb2], writes=[xbffmb], tag=f"bxtpe_{t}_{c}")
            for kc in range(8):
                p.emit("tensor",
                       (lambda c=c, kc=kc, xbf_fm=xbf_fm: lambda e: e.matmul(
                           ps_tiny[0:1, 384:512], onesP_bf[:],
                           xbf_fm[:, kc * TILE + c * 128:kc * TILE + (c + 1) * 128],
                           start=(kc == 0), stop=(kc == 7)))(),
                       reads=[xbffmb, consts], writes=[tinyb], tag=f"bnms_{t}_{c}_{kc}")
            p.emit("vector",
                   (lambda c=c, nmrow=nmrow: lambda e: e.tensor_scalar(
                       nmrow[:, c * 128:(c + 1) * 128], ps_tiny[0:1, 384:512], -1.0 / D, None, ALU.mult))(),
                   reads=[tinyb], writes=[nmrowb], tag=f"bnme_{t}_{c}")

        def mkck(t, c):
            bi = t % 2
            return dict(t=t, c=c, xt=x1sb[bi], xb=x1sbb[bi], x2t=x2o[bi], x2b=x2ob[bi],
                        xbf_fm=xbf_fmB[bi], xbffmb=xbffmbB[bi],
                        fac_sb=fac_sbP[bi][c], facb=facbP[bi][c],
                        stt_sb=stt_sbP[bi][c], sttb=sttbP[bi][c],
                        nmrow=nmrowP[bi], nmrowb=nmrowbP[bi])

        def emit_kproj(ck):
            t, c = ck["t"], ck["c"]
            xbf_fm, xbffmb, nmrow, nmrowb = ck["xbf_fm"], ck["xbffmb"], ck["nmrow"], ck["nmrowb"]
            for kc in range(8):
                for half in range(2):
                    p.emit("tensor",
                           (lambda c=c, kc=kc, half=half, xbf_fm=xbf_fm: lambda e: e.matmul(
                               psKAT[:, half * 512:(half + 1) * 512],
                               xbf_fm[:, kc * TILE + c * 128:kc * TILE + (c + 1) * 128],
                               Wk_sb[:, kc * D + half * 512:kc * D + (half + 1) * 512],
                               start=(kc == 0), stop=False))(),
                           reads=[xbffmb, wkb], writes=[psKATbH[half]], tag=f"bk_{t}_{c}_{kc}_{half}")
            for half in range(2):
                p.emit("tensor",
                       (lambda c=c, half=half, nmrow=nmrow: lambda e: e.matmul(
                           psKAT[:, half * 512:(half + 1) * 512], nmrow[:, c * 128:(c + 1) * 128],
                           cw_sb[:, D + half * 512:D + (half + 1) * 512], start=False, stop=True))(),
                       reads=[nmrowb, cwb], writes=[psKATbH[half]], tag=f"bkc_{t}_{c}_{half}")

        def _emit_gtladder(t, c, fac_sb, facb):
            p.emit("vector",
                   (lambda fac_sb=fac_sb: lambda e: e.tensor_scalar(gt_sb[:], ps_tiny[:, 0:NH], fac_sb[:, 3:4], None, ALU.mult))(),
                   reads=[tinyb, facb], writes=[gtb], tag=f"bze_{t}_{c}")
            p.emit("scalar", lambda e: e.activation(gt2_sb[:], gt_sb[:], AF.Abs),
                   reads=[gtb], writes=[gt2b], tag=f"bga_{t}_{c}")
            p.emit("scalar", lambda e: e.activation(gt2_sb[:], gt2_sb[:], AF.Exp, bias=0.0, scale=-1.0),
                   reads=[gt2b], writes=[gt2b], tag=f"bgxe_{t}_{c}")
            p.emit("scalar", lambda e: e.activation(gt2_sb[:], gt2_sb[:], AF.Ln, bias=1.0, scale=1.0),
                   reads=[gt2b], writes=[gt2b], tag=f"bgl_{t}_{c}")
            p.emit("vector",
                   lambda e: e.scalar_tensor_tensor(gt_sb[:], gt_sb[:], 0.0, gt2_sb[:], ALU.min, ALU.subtract),
                   reads=[gtb, gt2b], writes=[gtb], tag=f"bgs_{t}_{c}")
            if debug and t < 2:
                di = t * 2 + c
                p.emit("gpsimd", (lambda di=di: lambda e: e.dma_start(out=dbg_d[di, :, 0:NH], in_=gt_sb[:]))(),
                       reads=[gtb], dma=True, tag=f"dbggt{t}_{c}")

        def _emit_vproj(ck, half):
            t, c = ck["t"], ck["c"]
            xbf_fm, xbffmb = ck["xbf_fm"], ck["xbffmb"]
            fac_sb, facb = ck["fac_sb"], ck["facb"]
            stt_sb, sttb = ck["stt_sb"], ck["sttb"]
            nmrow, nmrowb = ck["nmrow"], ck["nmrowb"]
            for kc in range(8):
                p.emit("tensor",
                       (lambda c=c, kc=kc, half=half, xbf_fm=xbf_fm: lambda e: e.matmul(
                           ps_tiny[:, 0:512],
                           xbf_fm[:, kc * TILE + c * 128:kc * TILE + (c + 1) * 128],
                           Wv_sb[:, kc * D + half * 512:kc * D + (half + 1) * 512],
                           start=(kc == 0), stop=False))(),
                       reads=[xbffmb, wvb], writes=[tinyb], tag=f"bv_{t}_{c}_{half}_{kc}")
            p.emit("tensor",
                   (lambda c=c, half=half, nmrow=nmrow: lambda e: e.matmul(
                       ps_tiny[:, 0:512], nmrow[:, c * 128:(c + 1) * 128],
                       cw_sb[:, 2 * D + half * 512:2 * D + (half + 1) * 512], start=False, stop=True))(),
                   reads=[nmrowb, cwb], writes=[tinyb], tag=f"bvc_{t}_{c}_{half}")
            p.emit("scalar",
                   (lambda half=half, stt_sb=stt_sb: lambda e: e.activation(
                       sqscr[:, 0:512], ps_tiny[:, 0:512], AF.Square, accum_out=stt_sb[:, 2 + half:3 + half]))(),
                   reads=[tinyb], writes=[sqb, sttb], tag=f"bvss_{t}_{c}_{half}")
            p.emit("scalar",
                   (lambda c=c, half=half: lambda e: e.activation(
                       v_bf[:, c * D + half * 512:c * D + (half + 1) * 512], ps_tiny[:, 0:512], AF.Copy))(),
                   reads=[tinyb], writes=[vbfb], tag=f"bve_{t}_{c}_{half}")
            if half == 0:
                return
            p.emit("vector",
                   lambda e: e.tensor_add(stt_sb[:, 0:1], stt_sb[:, 2:3], stt_sb[:, 3:4]),
                   reads=[sttb], writes=[sttb], tag=f"bvsa_{t}_{c}")
            p.emit("vector",
                   lambda e: e.scalar_tensor_tensor(stt_sb[:, 1:2], stt_sb[:, 0:1], 1.0 / D, fac_sb[:, 2:3], ALU.mult, ALU.add),
                   reads=[sttb, facb], writes=[sttb], tag=f"bvf1_{t}_{c}")
            p.emit("scalar",
                   lambda e: e.activation(stt_sb[:, 1:2], stt_sb[:, 1:2], AF.Ln),
                   reads=[sttb], writes=[sttb], tag=f"bvf2_{t}_{c}")
            p.emit("scalar",
                   lambda e: e.activation(fac_sb[:, 4:5], stt_sb[:, 1:2], AF.Exp, bias=0.0, scale=-0.5),
                   reads=[sttb], writes=[facb], tag=f"bvf3_{t}_{c}")
            if debug and t < 2:
                di = t * 2 + c
                p.emit("gpsimd", (lambda di=di, c=c: lambda e: e.dma_start(out=dbg_bf[di + 8, :, 0:D], in_=v_bf[:, c * D:(c + 1) * D]))(),
                       reads=[vbfb], dma=True, tag=f"dbgv{t}_{c}")

        def _emit_zgt(ck):
            t, c = ck["t"], ck["c"]
            xbf_fm, xbffmb = ck["xbf_fm"], ck["xbffmb"]
            fac_sb, facb = ck["fac_sb"], ck["facb"]
            nmrow, nmrowb = ck["nmrow"], ck["nmrowb"]
            for kc in range(8):
                p.emit("tensor",
                       (lambda c=c, kc=kc, xbf_fm=xbf_fm: lambda e: e.matmul(
                           ps_tiny[:, 0:NH],
                           xbf_fm[:, kc * TILE + c * 128:kc * TILE + (c + 1) * 128],
                           Wgt_sb[:, kc * NH:(kc + 1) * NH],
                           start=(kc == 0), stop=False))(),
                       reads=[xbffmb, wgtb], writes=[tinyb], tag=f"bz_{t}_{c}_{kc}")
            p.emit("tensor",
                   (lambda c=c, nmrow=nmrow: lambda e: e.matmul(
                       ps_tiny[:, 0:NH], nmrow[:, c * 128:(c + 1) * 128],
                       cw_sb[:, 4 * D:4 * D + NH], start=False, stop=True))(),
                   reads=[nmrowb, cwb], writes=[tinyb], tag=f"bzc_{t}_{c}")
            _emit_gtladder(t, c, fac_sb, facb)

        def proj_block(ck, only_q=False):
            t, c = ck["t"], ck["c"]
            xbf_fm, xbffmb = ck["xbf_fm"], ck["xbffmb"]
            nmrow, nmrowb = ck["nmrow"], ck["nmrowb"]
            if not only_q:
                _emit_zgt(ck)
            for kc in range(8):
                for half in range(2):
                    p.emit("tensor",
                           (lambda c=c, kc=kc, half=half, xbf_fm=xbf_fm: lambda e: e.matmul(
                               psQG[:, half * 512:(half + 1) * 512],
                               xbf_fm[:, kc * TILE + c * 128:kc * TILE + (c + 1) * 128],
                               Wq_sb[:, kc * D + half * 512:kc * D + (half + 1) * 512],
                               start=(kc == 0), stop=False))(),
                           reads=[xbffmb, wqb], writes=[psQGb], tag=f"bq_{t}_{c}_{kc}_{half}")
            for half in range(2):
                p.emit("tensor",
                       (lambda c=c, half=half, nmrow=nmrow: lambda e: e.matmul(
                           psQG[:, half * 512:(half + 1) * 512], nmrow[:, c * 128:(c + 1) * 128],
                           cw_sb[:, half * 512:(half + 1) * 512], start=False, stop=True))(),
                       reads=[nmrowb, cwb], writes=[psQGb], tag=f"bqc_{t}_{c}_{half}")
            if only_q:
                return
            emit_kproj(ck)
            _emit_vproj(ck, 0)
            _emit_vproj(ck, 1)

        def chunk_compute(ck, nk):
            t, c = ck["t"], ck["c"]
            xt, xb, x2t, x2b = ck["xt"], ck["xb"], ck["x2t"], ck["x2b"]
            xbf_fm, xbffmb = ck["xbf_fm"], ck["xbffmb"]
            fac_sb, facb = ck["fac_sb"], ck["facb"]
            stt_sb, sttb = ck["stt_sb"], ck["sttb"]
            tc = t * 2 + c
            if tc % (tiles_per_seq * 2) == 0:
                p.emit("vector", lambda e: e.memset(S_sb[:], 0.0), reads=[Sbuf], writes=[Sbuf], tag=f"bS0_{tc}")
                p.emit("vector", lambda e: e.memset(S_bf[:], 0.0), reads=[Sbuf], writes=[Sbuf], tag=f"bS0b_{tc}")
            if debug and t < 2:
                di = t * 2 + c
                p.emit("gpsimd", (lambda di=di, fac_sb=fac_sb: lambda e: e.dma_start(out=dbg_d[di, :, 6 * NH:6 * NH + 8], in_=fac_sb[:]))(),
                       reads=[facb], dma=True, tag=f"dbgfac_{t}_{c}")
            p.emit("scalar",
                   lambda e: e.activation(sqscr[:], psQG[:], AF.Square, accum_out=stt_sb[:, 0:1]),
                   reads=[psQGb], writes=[sqb, sttb], tag=f"bqss_{t}_{c}")
            p.emit("tensor",
                   lambda e: e.matmul(ps_tiny[:, 32:48], cumu_f[:], gt_sb[:], start=True, stop=True),
                   reads=[gtb, consts], writes=[tinyb], tag=f"bcm_{t}_{c}")
            p.emit("tensor",
                   lambda e: e.matmul(ps_tiny[0:1, 48:64], onesP16_f[:], gt_sb[:], start=True, stop=True),
                   reads=[gtb, consts], writes=[tinyb], tag=f"bB_{t}_{c}")
            p.emit("scalar", lambda e: e.activation(E1_sb[:], ps_tiny[:, 32:48], AF.Exp),
                   reads=[tinyb], writes=[E1b], tag=f"bE1_{t}_{c}")
            p.emit("scalar", lambda e: e.activation(E2_sb[:], ps_tiny[:, 32:48], AF.Exp, bias=0.0, scale=-1.0),
                   reads=[tinyb], writes=[E2b], tag=f"bE2e_{t}_{c}")
            p.emit("vector", lambda e: e.tensor_copy(Braw_sb[:], ps_tiny[0:1, 48:64]),
                   reads=[tinyb], writes=[Brawb], tag=f"bBr_{t}_{c}")
            p.emit("tensor",
                   lambda e: e.matmul(ps_tiny[:, 64:80], ones1_f[:], Braw_sb[:], start=True, stop=True),
                   reads=[Brawb, consts], writes=[tinyb], tag=f"beBb_{t}_{c}")
            p.emit("vector", lambda e: e.tensor_copy(eBbc_sb[:], ps_tiny[:, 64:80]),
                   reads=[tinyb], writes=[eB2b], tag=f"beBbe_{t}_{c}")
            p.emit("scalar", lambda e: e.activation(eBexp_sb[:], eBbc_sb[:], AF.Exp),
                   reads=[eB2b], writes=[eB2b], tag=f"beB2x_{t}_{c}")
            if blevel < 5:
                return
            p.emit("vector",
                   lambda e: e.scalar_tensor_tensor(stt_sb[:, 1:2], stt_sb[:, 0:1], 1.0 / D, fac_sb[:, 2:3], ALU.mult, ALU.add),
                   reads=[sttb, facb], writes=[sttb], tag=f"bqf1_{t}_{c}")
            p.emit("scalar",
                   lambda e: e.activation(stt_sb[:, 1:2], stt_sb[:, 1:2], AF.Ln),
                   reads=[sttb], writes=[sttb], tag=f"bqf2_{t}_{c}")
            p.emit("scalar",
                   lambda e: e.activation(fac_sb[:, 0:1], stt_sb[:, 1:2], AF.Exp, bias=float(LOG_QSCALE), scale=-0.5),
                   reads=[sttb], writes=[facb], tag=f"bqf3_{t}_{c}")
            p.emit("vector",
                   (lambda c=c: lambda e: e.scalar_tensor_tensor(
                       qd_tm[:, c * D:(c + 1) * D].rearrange("p (h dh) -> p h dh", h=NH),
                       psQG[:].rearrange("p (h dh) -> p h dh", h=NH),
                       fac_sb[:, 0:1], bcast_in1(E1_sb[:][:, :, None], psQG[:].rearrange("p (h dh) -> p h dh", h=NH)),
                       ALU.mult, ALU.mult))(),
                   reads=[psQGb, facb, E1b], writes=[qdtmb], tag=f"bqd_{t}_{c}")
            p.emit("scalar",
                   lambda e: e.activation(sqscr[:], psKAT[:], AF.Square, accum_out=stt_sb[:, 0:1]),
                   reads=[psKATbH[0], psKATbH[1]], writes=[sqb, sttb], tag=f"bkss_{t}_{c}")
            p.emit("vector",
                   lambda e: e.scalar_tensor_tensor(stt_sb[:, 1:2], stt_sb[:, 0:1], 1.0 / D, fac_sb[:, 2:3], ALU.mult, ALU.add),
                   reads=[sttb, facb], writes=[sttb], tag=f"bkf1_{t}_{c}")
            p.emit("scalar",
                   lambda e: e.activation(stt_sb[:, 1:2], stt_sb[:, 1:2], AF.Ln),
                   reads=[sttb], writes=[sttb], tag=f"bkf2_{t}_{c}")
            p.emit("scalar",
                   lambda e: e.activation(fac_sb[:, 1:2], stt_sb[:, 1:2], AF.Exp, bias=0.0, scale=-0.5),
                   reads=[sttb], writes=[facb], tag=f"bkf3_{t}_{c}")
            p.emit("vector",
                   lambda e: e.tensor_mul(fac_sb[:, 1:2], fac_sb[:, 1:2], fac_sb[:, 4:5]),
                   reads=[facb], writes=[facb], tag=f"bkfv_{t}_{c}")
            p.emit("vector",
                   (lambda c=c: lambda e: e.scalar_tensor_tensor(
                       kd_tm[:, c * D:(c + 1) * D].rearrange("p (h dh) -> p h dh", h=NH),
                       psKAT[:].rearrange("p (h dh) -> p h dh", h=NH),
                       fac_sb[:, 1:2], bcast_in1(E2_sb[:][:, :, None], psKAT[:].rearrange("p (h dh) -> p h dh", h=NH)),
                       ALU.mult, ALU.mult))(),
                   reads=[psKATbH[0], psKATbH[1], facb, E2b], writes=[kdtmb], tag=f"bkd_{t}_{c}")
            if debug and t < 2:
                di = t * 2 + c
                p.emit("gpsimd", (lambda di=di: lambda e: e.dma_start(out=dbg_d[di, :, NH:2 * NH], in_=E1_sb[:]))(),
                       reads=[E1b], dma=True, tag=f"dbgE1{t}_{c}")
                p.emit("gpsimd", (lambda di=di: lambda e: e.dma_start(out=dbg_d[di, :, 2 * NH:3 * NH], in_=E2_sb[:]))(),
                       reads=[E2b], dma=True, tag=f"dbgE2{t}_{c}")
                p.emit("gpsimd", (lambda di=di: lambda e: e.dma_start(out=dbg_d[di, :, 4 * NH:4 * NH + 8], in_=eBexp_sb[:, 0:8]))(),
                       reads=[eB2b], dma=True, tag=f"dbgeB2{t}_{c}")
                p.emit("gpsimd", (lambda di=di: lambda e: e.dma_start(out=dbg_bf[di, :, 0:D], in_=qd_tm[:, 0:D] if di % 2 == 0 else qd_tm[:, D:2 * D]))(),
                       reads=[qdtmb], dma=True, tag=f"dbgqd{t}_{c}")
                p.emit("gpsimd", (lambda di=di, c=c: lambda e: e.dma_start(out=dbg_bf[di + 4, :, 0:D], in_=kd_tm[:, c * D:(c + 1) * D]))(),
                       reads=[kdtmb], dma=True, tag=f"dbgkd{t}_{c}")
                p.emit("gpsimd", (lambda di=di: lambda e: e.dma_start(out=dbg_d[di, 0:64, D:D + 1024], in_=S_sb[0:64, :]))(),
                       reads=[Sbuf], dma=True, tag=f"dbgS{t}_{c}")
            if blevel < 6:
                return
            for gi, (src_tm, srcb, dst_fm, dstb, tg) in enumerate(((qd_tm, qdtmb, qd_fm, qdfmb, "q"), (kd_tm, kdtmb, kd_fm, kdfmb, "k"))):
                for q4 in range(4):
                    pp = (gi * 4 + q4) % 2
                    pbank = ps_tp if pp == 0 else psO1
                    pbb = tpb2 if pp == 0 else psO1b
                    for hh in range(4):
                        h = q4 * 4 + hh
                        if pp == 0:
                            fn = (lambda src_tm=src_tm, c=c, h=h, hh=hh, pbank=pbank: lambda e: e.transpose(
                                pbank[0:64, hh * 128:(hh + 1) * 128],
                                src_tm[:, c * D + h * DH:c * D + (h + 1) * DH], ident_bf[:]))()
                        else:
                            fn = (lambda src_tm=src_tm, c=c, h=h, hh=hh, pbank=pbank: lambda e: e.matmul(
                                pbank[0:64, hh * 128:(hh + 1) * 128],
                                src_tm[:, c * D + h * DH:c * D + (h + 1) * DH], ident_bf[:],
                                start=True, stop=True))()
                        p.emit("tensor", fn, reads=[srcb, consts], writes=[pbb], tag=f"b{tg}t_{t}_{c}_{h}")
                    p.emit("scalar",
                           (lambda dst_fm=dst_fm, c=c, q4=q4, pbank=pbank: lambda e: e.copy(
                               dst_fm[0:64, (c * NH + q4 * 4) * 128:(c * NH + q4 * 4 + 4) * 128],
                               pbank[0:64, 0:512]))(),
                           reads=[pbb], writes=[dstb], tag=f"b{tg}te_{t}_{c}_{q4}")
            if blevel < 7:
                return
            for hh in range(8):
                p.emit("tensor",
                       (lambda c=c, hh=hh: lambda e: e.matmul(
                           psKAT[:, hh * 128:(hh + 1) * 128],
                           kd_fm[0:64, (c * NH + hh) * 128:(c * NH + hh + 1) * 128],
                           qd_fm[0:64, (c * NH + hh) * 128:(c * NH + hh + 1) * 128],
                           start=(hh % 4 == 0), stop=(hh % 4 == 3)))(),
                       reads=[kdfmb, qdfmb], writes=[psKATbH[hh // 4]], tag=f"bAT_{t}_{c}_{hh}")
            for hh in range(8):
                p.emit("vector",
                       (lambda hh=hh: lambda e: e.tensor_tensor(
                           A_sb[:, hh * 128:(hh + 1) * 128],
                           psKAT[:, hh * 128:(hh + 1) * 128],
                           tril_f[:], ALU.mult))(),
                       reads=[psKATbH[hh // 4], consts], writes=[Ab[hh]], tag=f"bAe_{t}_{c}_{hh}")
            for kc in range(8):
                for half in range(2):
                    p.emit("tensor",
                           (lambda c=c, kc=kc, half=half, xbf_fm=xbf_fm: lambda e: e.matmul(
                               psQG[:, half * 512:(half + 1) * 512],
                               xbf_fm[:, kc * TILE + c * 128:kc * TILE + (c + 1) * 128],
                               Wg_sb[:, kc * D + half * 512:kc * D + (half + 1) * 512],
                               start=(kc == 0), stop=False))(),
                           reads=[xbffmb, wgb], writes=[psQGb], tag=f"bgp_{t}_{c}_{kc}_{half}")
            for half in range(2):
                p.emit("tensor",
                       (lambda c=c, half=half, nmrow=ck["nmrow"]: lambda e: e.matmul(
                           psQG[:, half * 512:(half + 1) * 512], nmrow[:, c * 128:(c + 1) * 128],
                           cw_sb[:, 3 * D + half * 512:3 * D + (half + 1) * 512], start=False, stop=True))(),
                       reads=[ck["nmrowb"], cwb], writes=[psQGb], tag=f"bgc_{t}_{c}_{half}")
            p.emit("scalar",
                   lambda e: e.activation(g_sb[:], psQG[:], AF.Copy, scale=fac_sb[:, 3:4]),
                   reads=[psQGb, facb], writes=[gb], tag=f"bge2_{t}_{c}")
            p.emit("scalar", lambda e: e.activation(gscr[:], g_sb[:], AF.Abs),
                   reads=[gb], writes=[gscrb], tag=f"bsa_{t}_{c}")
            p.emit("scalar", lambda e: e.activation(gscr[:], gscr[:], AF.Exp, bias=0.0, scale=-1.0),
                   reads=[gscrb], writes=[gscrb], tag=f"bse_{t}_{c}")
            for hh in range(8):
                p.emit("tensor",
                       (lambda c=c, hh=hh: lambda e: e.matmul(
                           psO1[:, hh * DH:(hh + 1) * DH],
                           A_sb[:, hh * 128:(hh + 1) * 128],
                           v_bf[:, c * D + hh * DH:c * D + (hh + 1) * DH],
                           start=(hh == 0), stop=False))(),
                       reads=[Ab[hh], vbfb], writes=[psO1b], tag=f"boi_{t}_{c}_{hh}")
                p.emit("tensor",
                       (lambda c=c, hh=hh: lambda e: e.matmul(
                           psO1[:, hh * DH:(hh + 1) * DH],
                           qd_fm[0:64, (c * NH + hh) * 128:(c * NH + hh + 1) * 128],
                           S_bf[0:64, hh * DH:(hh + 1) * DH],
                           start=False, stop=(hh == 7)))(),
                       reads=[qdfmb, Sbuf], writes=[psO1b], tag=f"boc_{t}_{c}_{hh}")
            for hh in range(8):
                h = 8 + hh
                p.emit("tensor",
                       (lambda c=c, hh=hh, h=h: lambda e: e.matmul(
                           psKAT[:, hh * 128:(hh + 1) * 128],
                           kd_fm[0:64, (c * NH + h) * 128:(c * NH + h + 1) * 128],
                           qd_fm[0:64, (c * NH + h) * 128:(c * NH + h + 1) * 128],
                           start=(hh % 4 == 0), stop=(hh % 4 == 3)))(),
                       reads=[kdfmb, qdfmb], writes=[psKATbH[hh // 4]], tag=f"bAT_{t}_{c}_{h}")
            p.emit("scalar",
                   lambda e: e.copy(o_sb[:, 0:512], psO1[:, 0:512]),
                   reads=[psO1b], writes=[osbb], tag=f"boev0_{t}_{c}")
            p.emit("scalar",
                   lambda e: e.activation(oscr[:, 0:512], o_sb[:, 0:512], AF.Square),
                   reads=[osbb], writes=[oscrb], tag=f"bosq0_{t}_{c}")
            p.emit("vector",
                   lambda e: e.tensor_reduce(
                       oss_sb[:, 0:8][:, :, None], oscr[:, 0:512].rearrange("p (h dh) -> p h dh", h=8),
                       AX.X, ALU.add),
                   reads=[oscrb], writes=[ossb], tag=f"boss0_{t}_{c}")
            p.emit("scalar", lambda e: e.activation(gscr[:], gscr[:], AF.Ln, bias=1.0, scale=1.0),
                   reads=[gscrb], writes=[gscrb], tag=f"bsl_{t}_{c}")
            for hh in range(8):
                h = 8 + hh
                p.emit("vector",
                       (lambda hh=hh: lambda e: e.tensor_tensor(
                           A_sb[:, hh * 128:(hh + 1) * 128],
                           psKAT[:, hh * 128:(hh + 1) * 128],
                           tril_f[:], ALU.mult))(),
                       reads=[psKATbH[hh // 4], consts], writes=[Ab[hh]], tag=f"bAe_{t}_{c}_{h}")
            p.emit("vector",
                   lambda e: e.scalar_tensor_tensor(gscr2[:], g_sb[:], 0.0, gscr[:], ALU.min, ALU.subtract),
                   reads=[gb, gscrb], writes=[gscr2b], tag=f"bssm_{t}_{c}")
            for hh in range(8):
                p.emit("tensor",
                       (lambda c=c, hh=hh: lambda e: e.matmul(
                           psKV[0:64, hh * DH:(hh + 1) * DH],
                           kd_tm[:, c * D + hh * DH:c * D + (hh + 1) * DH],
                           v_bf[:, c * D + hh * DH:c * D + (hh + 1) * DH],
                           start=(hh == 0), stop=(hh == 7)))(),
                       reads=[kdtmb, vbfb], writes=[psKVb], tag=f"bkv_{t}_{c}_{hh}")
            if nk is not None:
                _emit_vproj(nk, 0)
            for hh in range(8):
                h = 8 + hh
                p.emit("tensor",
                       (lambda c=c, hh=hh, h=h: lambda e: e.matmul(
                           psO1[:, hh * DH:(hh + 1) * DH],
                           A_sb[:, hh * 128:(hh + 1) * 128],
                           v_bf[:, c * D + h * DH:c * D + (h + 1) * DH],
                           start=(hh == 0), stop=False))(),
                       reads=[Ab[hh], vbfb], writes=[psO1b], tag=f"boi_{t}_{c}_{h}")
                p.emit("tensor",
                       (lambda c=c, hh=hh, h=h: lambda e: e.matmul(
                           psO1[:, hh * DH:(hh + 1) * DH],
                           qd_fm[0:64, (c * NH + h) * 128:(c * NH + h + 1) * 128],
                           S_bf[0:64, h * DH:(h + 1) * DH],
                           start=False, stop=(hh == 7)))(),
                       reads=[qdfmb, Sbuf], writes=[psO1b], tag=f"boc_{t}_{c}_{h}")
            p.emit("vector",
                   lambda e: e.tensor_add(S_sb[0:64, 0:512], S_sb[0:64, 0:512], psKV[0:64, :]),
                   reads=[Sbuf, psKVb], writes=[Sbuf], tag=f"bSu2_{t}_{c}")
            for hh in range(8):
                h = 8 + hh
                p.emit("tensor",
                       (lambda c=c, hh=hh, h=h: lambda e: e.matmul(
                           psKV[0:64, hh * DH:(hh + 1) * DH],
                           kd_tm[:, c * D + h * DH:c * D + (h + 1) * DH],
                           v_bf[:, c * D + h * DH:c * D + (h + 1) * DH],
                           start=(hh == 0), stop=(hh == 7)))(),
                       reads=[kdtmb, vbfb], writes=[psKVb], tag=f"bkv_{t}_{c}_{h}")
            p.emit("scalar",
                   lambda e: e.copy(o_sb[:, 512:1024], psO1[:, 0:512]),
                   reads=[psO1b], writes=[osbb], tag=f"boev1_{t}_{c}")
            p.emit("scalar", lambda e: e.activation(gscr2[:], gscr2[:], AF.Exp),
                   reads=[gscr2b], writes=[gscr2b], tag=f"bsg_{t}_{c}")
            p.emit("vector", lambda e: e.tensor_mul(gsil[:], g_sb[:], gscr2[:]),
                   reads=[gb, gscr2b], writes=[gsilb], tag=f"bgsl_{t}_{c}")
            p.emit("vector",
                   lambda e: e.tensor_add(S_sb[0:64, 512:1024], S_sb[0:64, 512:1024], psKV[0:64, :]),
                   reads=[Sbuf, psKVb], writes=[Sbuf], tag=f"bSu2b_{t}_{c}")
            if blevel < 12:
                return
            p.emit("scalar",
                   lambda e: e.activation(oscr[:, 512:1024], o_sb[:, 512:1024], AF.Square),
                   reads=[osbb], writes=[oscrb], tag=f"bosq1_{t}_{c}")
            p.emit("vector",
                   lambda e: e.tensor_reduce(
                       oss_sb[:, 8:16][:, :, None], oscr[:, 512:1024].rearrange("p (h dh) -> p h dh", h=8),
                       AX.X, ALU.add),
                   reads=[oscrb], writes=[ossb], tag=f"boss1_{t}_{c}")
            p.emit("scalar",
                   lambda e: e.activation(rfac_sb[:], oss_sb[:], AF.Ln, bias=float(RMS_EPS), scale=1.0 / DH),
                   reads=[ossb], writes=[rfacb], tag=f"borf1_{t}_{c}")
            p.emit("scalar",
                   lambda e: e.activation(rfac_sb[:], rfac_sb[:], AF.Exp, bias=0.0, scale=-0.5),
                   reads=[rfacb], writes=[rfacb], tag=f"borf2_{t}_{c}")
            p.emit("vector",
                   lambda e: e.tensor_tensor(
                       oscr[:].rearrange("p (h dh) -> p h dh", h=NH),
                       o_sb[:].rearrange("p (h dh) -> p h dh", h=NH),
                       bcast_in1(rfac_sb[:][:, :, None], o_sb[:].rearrange("p (h dh) -> p h dh", h=NH)),
                       ALU.mult),
                   reads=[osbb, rfacb], writes=[oscrb], tag=f"bor_{t}_{c}")
            p.emit("vector",
                   lambda e: e.tensor_mul(og_tm[:], oscr[:], gsil[:]),
                   reads=[oscrb, gsilb], writes=[ogtmb], tag=f"bog_{t}_{c}")
            if debug and t < 2:
                di = t * 2 + c
                p.emit("gpsimd", (lambda di=di: lambda e: e.dma_start(out=dbg_bf[di + 12, :, 0:D], in_=og_tm[:]))(),
                       reads=[ogtmb], dma=True, tag=f"dbgog{t}_{c}")
                p.emit("gpsimd", (lambda di=di: lambda e: e.dma_start(out=dbg_d[di + 4, :, 0:D], in_=oscr[:]))(),
                       reads=[oscrb], dma=True, tag=f"dbgoscr{t}_{c}")
            if nk is not None:
                emit_kproj(nk)
                _emit_vproj(nk, 1)
            for kc in range(8):
                p.emit("tensor",
                       (lambda kc=kc: lambda e: e.transpose(
                           ps_tp[:, kc * 128:(kc + 1) * 128],
                           og_tm[:, kc * 128:(kc + 1) * 128], ident_bf[:]))(),
                       reads=[ogtmb, consts], writes=[tpb2], tag=f"bot_{t}_{c}_{kc}")
            p.emit("scalar",
                   lambda e: e.copy(og_fm[:], ps_tp[:]),
                   reads=[tpb2], writes=[ogfmb], tag=f"bote_{t}_{c}")
            if nk is not None:
                _emit_zgt(nk)
            for kc in range(8):
                for half in range(2):
                    p.emit("tensor",
                           (lambda kc=kc, half=half: lambda e: e.matmul(
                               psQG[:, half * 512:(half + 1) * 512],
                               og_fm[:, kc * 128:(kc + 1) * 128],
                               Wo_sb[:, kc * D + half * 512:kc * D + (half + 1) * 512],
                               start=(kc == 0), stop=(kc == 7)))(),
                           reads=[ogfmb, wob], writes=[psQGb], tag=f"bwo_{t}_{c}_{kc}_{half}")
            p.emit("vector",
                   (lambda c=c, xt=xt, x2t=x2t: lambda e: e.tensor_add(
                       x2t[:, c * D:c * D + 512], xt[:, c * D:c * D + 512], psQG[:, 0:512]))(),
                   reads=[psQGb, xb], writes=[x2b], tag=f"bx2_{t}_{c}")
            p.emit("vector",
                   (lambda c=c, xt=xt, x2t=x2t: lambda e: e.tensor_add(
                       x2t[:, c * D + 512:(c + 1) * D], xt[:, c * D + 512:(c + 1) * D], psQG[:, 512:1024]))(),
                   reads=[psQGb, xb], writes=[x2b], tag=f"bx2b_{t}_{c}")
            p.emit("vector",
                   (lambda x2t=x2t, c=c, tc=tc: lambda e: e.tensor_scalar(
                       sqscr[:], x2t[:, c * D:(c + 1) * D], 1.0, 0.0,
                       ALU.mult, ALU.add, accum_out=s13_sb[:, tc:tc + 1]))(),
                   reads=[x2b], writes=[sqb, stats3], tag=f"b31_{t}_{c}")
            p.emit("vector",
                   (lambda x2t=x2t, c=c, tc=tc: lambda e: e.scalar_tensor_tensor(
                       sqscr[:], x2t[:, c * D:(c + 1) * D], 0.0, x2t[:, c * D:(c + 1) * D],
                       ALU.add, ALU.mult, accum_out=s23_sb[:, tc:tc + 1]))(),
                   reads=[x2b], writes=[sqb, stats3], tag=f"b32_{t}_{c}")
            p.emit("vector",
                   lambda e: e.tensor_tensor(
                       S_sb[:].rearrange("p (g dh) -> p g dh", g=NH),
                       S_sb[:].rearrange("p (g dh) -> p g dh", g=NH),
                       bcast_in1(eBexp_sb[0:64, :][:, :, None], S_sb[:].rearrange("p (g dh) -> p g dh", g=NH)),
                       ALU.mult),
                   reads=[Sbuf, eB2b], writes=[Sbuf], tag=f"bSu1_{t}_{c}")
            p.emit("vector",
                   lambda e: e.tensor_copy(S_bf[:], S_sb[:]),
                   reads=[Sbuf], writes=[Sbuf], tag=f"bSu3_{t}_{c}")
            if nk is not None:
                proj_block(nk, only_q=True)

        for t in range(n_tiles + 1):
            if t < n_tiles:
                bi = t % 2
                xb, xt = x1sbb[bi], x1sb[bi]
                tok0 = t * TILE
                p.emit("sync",
                       (lambda xt=xt, tok0=tok0: lambda e: e.dma_start(
                           out=xt[:].rearrange("p (c f) -> p c f", c=2), in_=x1_d[tok0:tok0 + TILE, :].rearrange("(c p) f -> p c f", c=2)))(),
                       writes=[xb], dma=True, tag=f"bld{t}")
            if t > 0 and blevel >= 3:
                ck0, ck1 = mkck(t - 1, 0), mkck(t - 1, 1)
                chunk_compute(ck0, ck1)
            if t < n_tiles:
                for c in range(2):
                    ckl = mkck(t, c)
                    chunk_stats(t, c, ckl["xt"], ckl["xb"], ckl["xbf_fm"], ckl["xbffmb"],
                                ckl["fac_sb"], ckl["facb"], ckl["stt_sb"], ckl["sttb"],
                                ckl["nmrow"], ckl["nmrowb"])
            if blevel < 3:
                continue
            if t == 0:
                proj_block(mkck(0, 0))
                continue
            ck1 = mkck(t - 1, 1)
            nk2 = mkck(t, 0) if t < n_tiles else None
            chunk_compute(ck1, nk2)
            if blevel >= 14:
                tok0 = (t - 1) * TILE
                p.emit("gpsimd",
                       (lambda x2t=ck1["x2t"], tok0=tok0: lambda e: e.dma_start(
                           out=x2_d[tok0:tok0 + TILE, :].rearrange("(c p) f -> p c f", c=2), in_=x2t[:].rearrange("p (c f) -> p c f", c=2)))(),
                       reads=[ck1["x2b"]], dma=True, tag=f"bst{t - 1}")
        emit_stat_finish(s13_sb, s23_sb, stats3, m3_sb, r3_sb, stats3, n_cs)

    p.barrier()

    if "C" in phases:
        bw1t = bfv(0, 8 * HD)
        bw2t = bfv(8 * HD, 32 * D)
        bw1b, bw2b = Buf(), Buf()
        for kc in range(8):
            p.emit("sync",
                   (lambda kc=kc: lambda e: e.dma_start(
                       out=bw1t[:, kc * HD:(kc + 1) * HD], in_=bw1t_d[:, kc * HD:(kc + 1) * HD]))(),
                   writes=[bw1b], dma=True, tag=f"cbw1ld{kc}")
            p.emit("sync",
                   (lambda kc=kc: lambda e: e.dma_start(
                       out=bw2t[:, kc * 4 * D:(kc + 1) * 4 * D], in_=bw2t_d[:, kc * 4 * D:(kc + 1) * 4 * D]))(),
                   writes=[bw2b], dma=True, tag=f"cbw2ld{kc}")
        emit_ffn_phase(x2_d, out_d, m3_sb, r3_sb, stats3, bw1t, bw1b, bw2t, bw2b, "fc")

    final = Buf()
    last_store = p.eng_ops["gpsimd"][-1]
    last_store.inc = True

    p.finalize()

    with nc.Block() as blk:
        blk.sync(lambda e: p.replay_engine("sync", e))
        blk.tensor(lambda e: p.replay_engine("tensor", e))
        blk.vector(lambda e: p.replay_engine("vector", e))
        blk.scalar(lambda e: p.replay_engine("scalar", e))

        def _gp(e):
            p.replay_engine("gpsimd", e)
            for i in range(NSLOT):
                fc = p.final_counts.get(f"gpsimd_dma{i}", 0)
                if fc > 0:
                    e.wait_ge(p.sems[f"gpsimd_dma{i}"], fc)
        blk.gpsimd(_gp)

    stP.close()
    return nc


def make_consts():
    return {
        "c_identbf": np.eye(128, dtype=np.float32).astype(BFNP),
        "c_identf": np.eye(128, dtype=np.float32),
        "c_tril": np.tril(np.ones((128, 128), dtype=np.float32)).T.copy(),
        "c_cumu": (np.triu(np.ones((128, 128), dtype=np.float32)) / 16.0),
    }


_NC_CACHE = {}


def kernel(L0_attn_x, L0_attn_Wq, L0_attn_Wk, L0_attn_Wv, L0_attn_Wg, L0_attn_Wgt,
           L0_attn_Wo, L0_ffn_w1, L0_ffn_b1, L0_ffn_w2, L0_ffn_b2):
    x = np.ascontiguousarray(np.asarray(L0_attn_x, dtype=np.float32))
    B, T, _ = x.shape
    n_cores = 8
    per = B // n_cores
    n_tok = per * T
    if n_tok not in _NC_CACHE:
        _NC_CACHE[n_tok] = build_program(n_tok)
    nc = _NC_CACHE[n_tok]
    consts = make_consts()
    weights = {
        "wq": np.ascontiguousarray(np.asarray(L0_attn_Wq, dtype=np.float32)),
        "wk": np.ascontiguousarray(np.asarray(L0_attn_Wk, dtype=np.float32)),
        "wv": np.ascontiguousarray(np.asarray(L0_attn_Wv, dtype=np.float32)),
        "wg": np.ascontiguousarray(np.asarray(L0_attn_Wg, dtype=np.float32)),
        "wgt": np.ascontiguousarray(np.asarray(L0_attn_Wgt, dtype=np.float32)),
        "wo": np.ascontiguousarray(np.asarray(L0_attn_Wo, dtype=np.float32)),
        "w1": np.ascontiguousarray(np.asarray(L0_ffn_w1, dtype=np.float32)),
        "b1": np.ascontiguousarray(np.asarray(L0_ffn_b1, dtype=np.float32)),
        "w2": np.ascontiguousarray(np.asarray(L0_ffn_w2, dtype=np.float32)),
        "b2": np.ascontiguousarray(np.asarray(L0_ffn_b2, dtype=np.float32)),
    }
    in_maps = []
    for c in range(n_cores):
        im = {"x": np.ascontiguousarray(x[c * per:(c + 1) * per].reshape(n_tok, D))}
        im.update(weights)
        im.update(consts)
        in_maps.append(im)
    res = run_bass_kernel_spmd(nc, in_maps, core_ids=list(range(n_cores)))
    outs = [res.results[c]["out"].reshape(per, T, D) for c in range(n_cores)]
    return np.concatenate(outs, axis=0)
